# version 19
# baseline (speedup 1.0000x reference)
import contextlib
import numpy as np
import ml_dtypes
import concourse.bass as bass
import concourse.mybir as mybir
from concourse.bass_utils import run_bass_kernel_spmd

F32 = mybir.dt.float32
BF16 = mybir.dt.bfloat16
I8 = mybir.dt.int8
AF = mybir.ActivationFunctionType
ALU = mybir.AluOpType

NCORES = 8
T = 2048
DM = 2048
NBLK = 4
TB = 512
NK = 16
CH = 128
EPS = 1e-5
MASKVAL = -1e9

ENGS = ("sy", "gp", "pe", "dv", "ac")
SEMS = ("pe", "dv", "ac", "gpc", "gpx",
        "syc", "sya0", "sya1", "sya2", "sya3", "syx0", "syx1", "syst",
        "gp0", "gp1", "gp2", "gp3")
DEBUG = False


class Tape:
    def __init__(self):
        self.tape = {k: [] for k in ENGS}
        self.cnt = {k: 0 for k in SEMS}

    def emit(self, ek, fn, waits=(), inc=0, incs=None):
        w = [(sk, v) for (sk, v) in waits if v and v > 0]
        if incs is None:
            incs = [(ek, inc)] if inc else []
        self.tape[ek].append((fn, w, incs))
        ret = 0
        for sk, amt in incs:
            self.cnt[sk] += amt
            ret = self.cnt[sk]
        return ret

    def snap(self, ek):
        return self.cnt[ek]


def build_program():
    nc = bass.Bass("TRN2", target_bir_lowering=False, debug=False)

    xst = nc.dram_tensor("xst", [NBLK, NK, 128, TB], F32, kind="ExternalInput").ap()
    xpst = nc.dram_tensor("xpst", [NBLK, NK, 128, TB], F32, kind="ExternalInput").ap()
    wst = nc.dram_tensor("wst", [65, 128, NK, 128], F32, kind="ExternalInput").ap()
    wost = nc.dram_tensor("wost", [16, 128, NK, 128], F32, kind="ExternalInput").ap()
    wgst = nc.dram_tensor("wgst", [64, 128, NK, 128], I8, kind="ExternalInput").ap()
    wust = nc.dram_tensor("wust", [64, 128, NK, 128], I8, kind="ExternalInput").ap()
    wdst = nc.dram_tensor("wdst", [16, 4, 128, NK, 128], I8, kind="ExternalInput").ap()
    gam = nc.dram_tensor("gam", [128, 16], F32, kind="ExternalInput").ap()
    bet = nc.dram_tensor("bet", [128, 16], F32, kind="ExternalInput").ap()
    sgm = nc.dram_tensor("sgm", [128, 64], F32, kind="ExternalInput").ap()
    sum_ = nc.dram_tensor("sum_", [128, 64], F32, kind="ExternalInput").ap()
    sdm = nc.dram_tensor("sdm", [128, 16], F32, kind="ExternalInput").ap()
    d2m = nc.dram_tensor("d2m", [128, 32], F32, kind="ExternalInput").ap()
    nege32 = nc.dram_tensor("nege32", [32, 32 * 128], F32, kind="ExternalInput").ap()
    identb = nc.dram_tensor("identb", [128, 128], BF16, kind="ExternalInput").ap()
    identf = nc.dram_tensor("identf", [32, 32], F32, kind="ExternalInput").ap()
    maskt = nc.dram_tensor("maskt", [128, 128], BF16, kind="ExternalInput").ap()
    onesb = nc.dram_tensor("onesb", [128, 1], BF16, kind="ExternalInput").ap()
    onesr = nc.dram_tensor("onesr", [1, 128], F32, kind="ExternalInput").ap()
    fconst = nc.dram_tensor("fconst", [128, 8], F32, kind="ExternalInput").ap()
    outst = nc.dram_tensor("outst", [NBLK, 16, 128, TB], F32, kind="ExternalOutput").ap()
    if DEBUG:
        dxn = nc.dram_tensor("dxn", [128, 8192], BF16, kind="ExternalOutput").ap()
        dlap = nc.dram_tensor("dlap", [32, 512], F32, kind="ExternalOutput").ap()
        dncf = nc.dram_tensor("dncf", [32, 512], F32, kind="ExternalOutput").ap()
        dnct = nc.dram_tensor("dnct", [128, 128], F32, kind="ExternalOutput").ap()
        dyg = nc.dram_tensor("dyg", [128, 8192], BF16, kind="ExternalOutput").ap()
        dyn = nc.dram_tensor("dyn", [128, 8192], BF16, kind="ExternalOutput").ap()
        dss = nc.dram_tensor("dss", [128, 8192], BF16, kind="ExternalOutput").ap()
        dstf = nc.dram_tensor("dstf", [64, 2048], F32, kind="ExternalOutput").ap()
        dfuse = nc.dram_tensor("dfuse", [128, 4096], BF16, kind="ExternalOutput").ap()
        da1 = nc.dram_tensor("da1", [128, 32768], BF16, kind="ExternalOutput").ap()

    with contextlib.ExitStack() as stack:
        sb = lambda n, s, d: stack.enter_context(nc.sbuf_tensor(n, s, d)).ap()

        A1 = sb("A1", [128, 32768], BF16)
        A2 = sb("A2", [128, 8192], BF16)
        A3 = sb("A3", [128, 2048], F32)
        XN = sb("XN", [128, 8192], BF16)
        YG = sb("YG", [128, 8192], BF16)
        SS = sb("SS", [128, 8192], BF16)
        GS = sb("GS", [128, 1024], F32)
        GS2 = sb("GS2", [128, 512], F32)
        XR = sb("XR", [128, 1024], F32)
        X2 = sb("X2", [128, 1024], BF16)
        MT = sb("MT", [128, 512], F32)
        STt = sb("STt", [128, 512], BF16)
        DE = sb("DE", [128, 512], F32)
        CDt = sb("CDt", [64, 512], BF16)
        BW = sb("BW", [128, 256], BF16)
        XT = sb("XT", [128, 512], BF16)
        TMP = sb("TMP", [128, 512], F32)
        NCF = sb("NCF", [32, 512], F32)
        LAP = sb("LAP", [32, 512], F32)
        NCT = sb("NCT", [128, 128], F32)
        WCOL = sb("WCOL", [128, 4], F32)
        LCOL = sb("LCOL", [64, 4], F32)
        ONES32 = sb("ONES32", [32, 128], F32)
        ROWS = sb("ROWS", [1, 3584], F32)
        STF = sb("STF", [64, 2048], F32)
        STB = sb("STB", [64, 2048], BF16)
        CN32 = sb("CN32", [32, 32 * 128], F32)
        CIDB = sb("CIDB", [128, 128], BF16)
        CIDF = sb("CIDF", [32, 32], F32)
        CMSK = sb("CMSK", [128, 128], BF16)
        CONB = sb("CONB", [128, 1], BF16)
        CONR = sb("CONR", [1, 128], F32)
        CGAM = sb("CGAM", [128, 16], F32)
        CBET = sb("CBET", [128, 16], F32)
        CSG = sb("CSG", [128, 64], F32)
        CSU = sb("CSU", [128, 64], F32)
        CSD = sb("CSD", [128, 16], F32)
        CD2 = sb("CD2", [128, 32], F32)
        CFC = sb("CFC", [128, 8], F32)
        for i, v in enumerate([EPS, 1.0 / DM, -0.5, -1.0, 0.0, 1.0]):
            nc.const_aps.register(F32, v, CFC[:, i:i + 1])

        PS = [stack.enter_context(nc.psum_tensor(f"ps{i}", [128, 512], F32)).ap()
              for i in range(8)]
        PSB = {i: PS[i].bitcast(BF16) for i in range(8)}

        SEM = {k: stack.enter_context(nc.semaphore(f"sem_{k}")) for k in SEMS}
        block = stack.enter_context(nc.Block())

        t = Tape()
        dep = {}
        ps_dep = {i: None for i in range(8)}
        a2_dep = {i: None for i in range(4)}
        xn_guard = [None]
        a3_guard = {i: [] for i in range(4)}
        xr_guard = {0: None, 1: None}
        ss_guard = [None]
        yg_guard = [None]

        for (dst, src) in [(CN32, nege32), (CIDB, identb), (CIDF, identf),
                           (CMSK, maskt), (CONB, onesb), (CONR, onesr),
                           (CGAM, gam), (CBET, bet), (CSG, sgm), (CSU, sum_),
                           (CSD, sdm), (CD2, d2m), (CFC, fconst)]:
            t.emit("sy", lambda e, d=dst, s=src: e.dma_start(out=d, in_=s),
                   incs=[("syc", 16)])
        consts_sy = t.snap("syc")
        for ek in ("pe", "dv", "ac", "gp"):
            t.emit(ek, lambda e: None, waits=[("syc", consts_sy)])
        t.emit("dv", lambda e: e.memset(ONES32, 1.0), inc=1)
        t.emit("dv", lambda e: e.memset(STF, 0.0), inc=1)
        t.emit("dv", lambda e: e.memset(STB, 0.0), inc=1)

        def rows_pipeline(wait_pe):
            mu = ROWS[:, 0:512]; e2 = ROWS[:, 512:1024]; mu2 = ROWS[:, 1024:1536]
            var = ROWS[:, 1536:2048]; lnv = ROWS[:, 2048:2560]
            r1 = ROWS[:, 2560:3072]; r0n = ROWS[:, 3072:3584]
            t.emit("ac", lambda e: e.activation(out=mu, in_=PS[2][0:1, :], func=AF.Copy,
                                                scale=1.0 / DM), waits=[("pe", wait_pe)], inc=1)
            t.emit("ac", lambda e: e.activation(out=e2, in_=PS[3][0:1, :], func=AF.Copy,
                                                scale=1.0 / DM), inc=1)
            t.emit("ac", lambda e: e.activation(out=mu2, in_=mu, func=AF.Square), inc=1)
            ac_r = t.snap("ac")
            t.emit("dv", lambda e: e.tensor_sub(out=var, in0=e2, in1=mu2),
                   waits=[("ac", ac_r)], inc=1)
            dv_r = t.snap("dv")
            t.emit("ac", lambda e: e.activation(out=lnv, in_=var, func=AF.Ln, bias=EPS),
                   waits=[("dv", dv_r)], inc=1)
            t.emit("ac", lambda e: e.activation(out=r1, in_=lnv, func=AF.Exp, scale=-0.5),
                   inc=1)
            ac_r1 = t.snap("ac")
            t.emit("dv", lambda e: e.scalar_tensor_tensor(
                out=r0n, in0=mu, scalar=-1.0, in1=r1, op0=ALU.mult, op1=ALU.mult),
                waits=[("ac", ac_r1)], inc=1)
            dv_r0 = t.snap("dv")
            return ac_r1, dv_r0, r1, r0n

        def ln_load(stage, blk):
            return t.emit("gp", lambda e, blk=blk: e.dma_start(
                out=XN.rearrange("p (k t) -> p k t", k=NK),
                in_=stage[blk].rearrange("k p t -> p k t")),
                waits=([xn_guard[0]] if xn_guard[0] else []), incs=[("gpx", 16)])

        def ln_stats(gp_x):
            for k in range(NK):
                xk = XN[:, k * TB:(k + 1) * TB]
                x2s = X2[:, (k % 2) * TB:(k % 2 + 1) * TB]
                t.emit("ac", lambda e, xk=xk, x2s=x2s: e.activation(
                    out=x2s, in_=xk, func=AF.Square),
                    waits=[("gpx", gp_x)]
                          + ([dep.get(("x2", k % 2))] if dep.get(("x2", k % 2)) else []),
                    inc=1)
                ack = t.snap("ac")
                t.emit("pe", lambda e, xk=xk, k=k: e.matmul(
                    out=PS[2][0:1, :], lhsT=CONB, rhs=xk,
                    start=(k == 0), stop=(k == NK - 1)),
                    waits=[("gpx", gp_x)] + ([ps_dep[2]] if k == 0 and ps_dep[2] else []),
                    inc=0)
                t.emit("pe", lambda e, x2s=x2s, k=k: e.matmul(
                    out=PS[3][0:1, :], lhsT=CONB, rhs=x2s,
                    start=(k == 0), stop=(k == NK - 1)),
                    waits=[("ac", ack)] + ([ps_dep[3]] if k == 0 and ps_dep[3] else []),
                    inc=1)
                dep[("x2", k % 2)] = ("pe", t.snap("pe"))
            pe_stats = t.snap("pe")
            ac_r1, dv_r0, r1, r0n = rows_pipeline(pe_stats)
            t.emit("pe", lambda e, r1=r1: e.matmul(out=PS[2], lhsT=CONR, rhs=r1,
                                                   start=True, stop=True),
                   waits=[("ac", ac_r1)], inc=0)
            t.emit("pe", lambda e, r0n=r0n: e.matmul(out=PS[3], lhsT=CONR, rhs=r0n,
                                                     start=True, stop=True),
                   waits=[("dv", dv_r0)], inc=1)
            pe_bc = t.snap("pe")
            for k in range(NK):
                xk = XN[:, k * TB:(k + 1) * TB]
                t.emit("dv", lambda e, xk=xk: e.tensor_mul(out=xk, in0=xk, in1=PS[2]),
                       waits=[("pe", pe_bc)] if k == 0 else [], inc=0)
                t.emit("dv", lambda e, xk=xk: e.tensor_add(out=xk, in0=xk, in1=PS[3]),
                       inc=(1 if k == NK - 1 else 0))
            dv_norm = t.snap("dv")
            ps_dep[2] = ("dv", dv_norm)
            ps_dep[3] = ("dv", dv_norm)
            return dv_norm

        def ln_block(stage, blk):
            return ln_stats(ln_load(stage, blk))

        def proj_strip(cc, slot, kind, xn_dep):
            bank = cc % 2
            gp_l = t.emit("gp", lambda e, cc=cc, slot=slot: e.dma_start(
                out=A2[:, slot * 2048:(slot + 1) * 2048].rearrange("p (k c) -> p k c", k=NK),
                in_=wst[cc]),
                waits=[a2_dep[slot]] if a2_dep[slot] else [], incs=[(f"gp{slot}", 16)])
            for k in range(NK):
                t.emit("pe", lambda e, k=k, slot=slot, bank=bank: e.matmul(
                    out=PS[bank],
                    lhsT=A2[:, slot * 2048 + k * 128:slot * 2048 + (k + 1) * 128],
                    rhs=XN[:, k * TB:(k + 1) * TB],
                    start=(k == 0), stop=(k == NK - 1)),
                    waits=([(f"gp{slot}", gp_l), ("dv", xn_dep)]
                           + ([ps_dep[bank]] if ps_dep[bank] else [])) if k == 0 else [],
                    inc=(1 if k == NK - 1 else 0))
            pe_mm = t.snap("pe")
            a2_dep[slot] = ("pe", pe_mm)
            if kind == "x":
                dst = A1[:, cc * TB:(cc + 1) * TB]
            elif kind == "z":
                dst = A1[:, 16384 + (cc - 16) * TB:16384 + (cc - 15) * TB]
            elif kind == "b":
                dst = A1[:, 8192 + (cc - 32) * TB:8192 + (cc - 31) * TB]
            elif kind == "c":
                dst = A1[:, 24576 + (cc - 48) * TB:24576 + (cc - 47) * TB]
            if kind in ("x", "b"):
                t.emit("dv", lambda e, dst=dst, bank=bank: e.tensor_copy(out=dst, in_=PS[bank]),
                       waits=[("pe", pe_mm)], inc=1)
                ps_dep[bank] = ("dv", t.snap("dv"))
            elif kind == "z":
                t.emit("ac", lambda e, dst=dst, bank=bank: e.activation(
                    out=dst, in_=PS[bank], func=AF.Silu), waits=[("pe", pe_mm)], inc=1)
                ps_dep[bank] = ("ac", t.snap("ac"))
            elif kind == "c":
                t.emit("ac", lambda e, dst=dst, bank=bank: e.activation(
                    out=dst, in_=PS[bank], func=AF.Copy), waits=[("pe", pe_mm)], inc=1)
                ps_dep[bank] = ("ac", t.snap("ac"))
            elif kind == "a":
                t.emit("ac", lambda e, bank=bank: e.activation(
                    out=LAP, in_=PS[bank][0:32, :], func=AF.Exp),
                    waits=[("pe", pe_mm)], inc=1)
                ps_dep[bank] = ("ac", t.snap("ac"))

        def xf_sl(h, tau):
            o = (h // 2) * TB + tau * CH
            return A1[(h % 2) * 64:(h % 2) * 64 + 64, o:o + CH]

        def b_sl(h, tau):
            o = 8192 + (h // 2) * TB + tau * CH
            return A1[(h % 2) * 64:(h % 2) * 64 + 64, o:o + CH]

        def z_sl(h, tau):
            o = 16384 + (h // 2) * TB + tau * CH
            return A1[(h % 2) * 64:(h % 2) * 64 + 64, o:o + CH]

        def c_sl(h, tau):
            o = 24576 + (h // 2) * TB + tau * CH
            return A1[(h % 2) * 64:(h % 2) * 64 + 64, o:o + CH]

        def b_pair(hp, tau):
            o = 8192 + hp * TB + tau * CH
            return A1[:, o:o + CH]

        def x_pair(hp, tau):
            o = hp * TB + tau * CH
            return A1[:, o:o + CH]

        P_H = [4, 5, 0, 1]
        T_H = [6, 7, 2, 3]

        def ssd_chunk(tau, summary_only, proj_dv, la_ac=None, z_ac=None):
            if la_ac is None:
                la_ac = t.snap("ac")
            ncf_sl = NCF[:, tau * CH:(tau + 1) * CH]
            t.emit("dv", lambda e, ncf_sl=ncf_sl, tau=tau: e.tensor_tensor_scan(
                out=ncf_sl, data0=ONES32, data1=LAP[:, tau * CH:(tau + 1) * CH],
                initial=0.0, op0=ALU.mult, op1=ALU.add),
                waits=[("ac", la_ac)] + ([dep.get(("ncf", tau))] if dep.get(("ncf", tau)) else []),
                inc=1)
            dv_sc = t.snap("dv")
            t.emit("pe", lambda e, ncf_sl=ncf_sl: e.transpose(
                out=PS[6][:, 448:480], in_=ncf_sl, identity=CIDF),
                waits=[("dv", dv_sc)]
                      + ([dep.get(("trb", 6))] if dep.get(("trb", 6)) else [])
                      + ([dep.get("nct_evac")] if dep.get("nct_evac") else []),
                inc=1)
            pe_tr = t.snap("pe")
            nct_sl = NCT[:, tau * 32:(tau + 1) * 32]
            t.emit("ac", lambda e, nct_sl=nct_sl: e.copy(out=nct_sl, in_=PS[6][:, 448:480]),
                   waits=[("pe", pe_tr)], inc=1)
            dep["nct_evac"] = ("ac", t.snap("ac"))

            ctx = {}

            def front(i):
                h = i
                hp, j = i // 2, i % 2
                jp = h % 4
                pb_ = P_H[jp]
                tb_ = T_H[hp % 4]
                pbank = PSB[tb_]
                c = {"h": h, "hp": hp, "j": j, "jp": jp, "pb_": pb_, "tb_": tb_,
                     "pbank": pbank}
                if j == 0:
                    trw = ([dep.get(("trb", tb_))] if dep.get(("trb", tb_)) else [])
                    if tb_ == 6:
                        trw = trw + [dep["nct_evac"]]
                    t.emit("pe", lambda e, hp=hp, tau=tau, pbank=pbank: e.transpose(
                        out=pbank[:, 0:128], in_=b_pair(hp, tau), identity=CIDB),
                        waits=trw + [("dv", proj_dv)], inc=0)
                    t.emit("pe", lambda e, hp=hp, tau=tau, pbank=pbank: e.transpose(
                        out=pbank[:, 128:256], in_=x_pair(hp, tau), identity=CIDB), inc=1)
                    pe_trp = t.snap("pe")
                    t.emit("ac", lambda e, hp=hp, pbank=pbank: e.copy(
                        out=XT[:, (hp % 4) * 128:(hp % 4) * 128 + 128],
                        in_=pbank[:, 128:256]),
                        waits=[("pe", pe_trp)]
                              + ([dep.get(("xt_use", hp % 4))] if dep.get(("xt_use", hp % 4))
                                 else []),
                        inc=1)
                    dep[("acxt", hp)] = t.snap("ac")
                c["ac_xt"] = dep[("acxt", hp)]
                if not summary_only:
                    t.emit("pe", lambda e, h=h, tau=tau, pb_=pb_: e.matmul(
                        out=PS[pb_][:, 192:320], lhsT=b_sl(h, tau),
                        rhs=c_sl(h, tau), start=True, stop=True),
                        waits=([ps_dep[pb_]] if ps_dep[pb_] else [])
                              + [("dv", proj_dv)], inc=0)
                t.emit("pe", lambda e, h=h, tau=tau, pb_=pb_: e.matmul(
                    out=PS[pb_][:, 320:448],
                    lhsT=CN32[:, h * 128:(h + 1) * 128],
                    rhs=NCF[:, tau * CH:(tau + 1) * CH],
                    start=True, stop=summary_only),
                    waits=[ps_dep[pb_]] if ps_dep[pb_] else [],
                    inc=1)
                pe_b1 = t.snap("pe")
                t.emit("ac", lambda e, jp=jp, pb_=pb_, tau=tau, h=h: e.activation(
                    out=WCOL[:, jp:jp + 1], in_=PS[pb_][:, 447:448], func=AF.Exp,
                    bias=NCT[:, tau * 32 + h:tau * 32 + h + 1]),
                    waits=[("pe", pe_b1)], inc=0)
                t.emit("ac", lambda e, jp=jp, pb_=pb_: e.activation(
                    out=LCOL[:, jp:jp + 1], in_=PS[pb_][0:64, 447:448], func=AF.Exp),
                    inc=(1 if summary_only else 0))
                if not summary_only:
                    t.emit("ac", lambda e, jp=jp, j=j, pb_=pb_: e.activation(
                        out=DE[j * 64:j * 64 + 64, jp * 128:jp * 128 + 128],
                        in_=PS[pb_][0:64, 320:448], func=AF.Exp), inc=1)
                c["ac_a"] = t.snap("ac")
                ctx[i] = c

            def mid(i):
                c = ctx[i]
                h, hp, j, jp = c["h"], c["hp"], c["j"], c["jp"]
                pb_, tb_, pbank = c["pb_"], c["tb_"], c["pbank"]
                if not summary_only:
                    t.emit("pe", lambda e, pb_=pb_: e.matmul(
                        out=PS[pb_][:, 320:448], lhsT=CIDB, rhs=CMSK,
                        start=False, stop=True),
                        waits=[("ac", c["ac_a"])], inc=1)
                    pe_msk = t.snap("pe")
                    t.emit("ac", lambda e, jp=jp, pb_=pb_, tau=tau, h=h: e.activation(
                        out=MT[:, jp * 128:jp * 128 + 128], in_=PS[pb_][:, 320:448],
                        func=AF.Exp, bias=NCT[:, tau * 32 + h:tau * 32 + h + 1]),
                        waits=[("pe", pe_msk)], inc=1)
                    ac_b = t.snap("ac")
                t.emit("dv", lambda e, jp=jp, j=j, pbank=pbank: e.tensor_scalar(
                    out=BW[:, jp * 64:jp * 64 + 64],
                    in0=pbank[:, j * 64:j * 64 + 64],
                    scalar1=WCOL[:, jp:jp + 1], scalar2=None, op0=ALU.mult),
                    waits=[("ac", c["ac_a"])], inc=(1 if summary_only else 0))
                if not summary_only:
                    t.emit("dv", lambda e, h=h, tau=tau, jp=jp, j=j: e.tensor_mul(
                        out=CDt[:, jp * 128:jp * 128 + 128], in0=c_sl(h, tau),
                        in1=DE[j * 64:j * 64 + 64, jp * 128:jp * 128 + 128]),
                        waits=([dep.get(("cd", jp))] if dep.get(("cd", jp)) else []),
                        inc=0)
                    t.emit("dv", lambda e, jp=jp, pb_=pb_: e.tensor_mul(
                        out=STt[:, jp * 128:jp * 128 + 128],
                        in0=PS[pb_][:, 192:320],
                        in1=MT[:, jp * 128:jp * 128 + 128]),
                        waits=[("ac", ac_b)], inc=1)
                c["dv_s"] = t.snap("dv")

            def back(i):
                c = ctx.pop(i)
                h, hp, j, jp = c["h"], c["hp"], c["j"], c["jp"]
                pb_, tb_ = c["pb_"], c["tb_"]
                dv_s, ac_xt = c["dv_s"], c["ac_xt"]
                stb_dep = dep.get(("stb", h))
                if not summary_only:
                    t.emit("pe", lambda e, jp=jp, hp=hp, j=j, pb_=pb_: e.matmul(
                        out=PS[pb_][0:64, 0:128],
                        lhsT=XT[:, (hp % 4) * 128 + j * 64:(hp % 4) * 128 + j * 64 + 64],
                        rhs=STt[:, jp * 128:jp * 128 + 128],
                        start=True, stop=False),
                        waits=[("dv", dv_s), ("ac", ac_xt)]
                              + ([stb_dep] if stb_dep else []), inc=0)
                    t.emit("pe", lambda e, h=h, jp=jp, pb_=pb_: e.matmul(
                        out=PS[pb_][0:64, 0:128],
                        lhsT=STB[:, h * 64:(h + 1) * 64],
                        rhs=CDt[:, jp * 128:jp * 128 + 128],
                        start=False, stop=True), inc=0)
                t.emit("pe", lambda e, jp=jp, hp=hp, j=j, pb_=pb_: e.matmul(
                    out=PS[pb_][0:64, 128:192],
                    lhsT=BW[:, jp * 64:jp * 64 + 64],
                    rhs=XT[:, (hp % 4) * 128 + j * 64:(hp % 4) * 128 + j * 64 + 64],
                    start=True, stop=True),
                    waits=([("dv", dv_s), ("ac", ac_xt)]
                           + ([dep.get(("stf", h))] if dep.get(("stf", h)) else []))
                          if summary_only else [],
                    inc=1)
                pe_b2 = t.snap("pe")
                if j == 1:
                    dep[("xt_use", hp % 4)] = ("pe", pe_b2)
                    dep[("trb", tb_)] = ("dv", dv_s)
                if not summary_only:
                    t.emit("dv", lambda e, h=h, tau=tau, jp=jp, j=j, pb_=pb_:
                           e.scalar_tensor_tensor(
                               out=TMP[j * 64:j * 64 + 64, jp * 128:jp * 128 + 128],
                               in0=xf_sl(h, tau),
                               scalar=CD2[j * 64:j * 64 + 64, h:h + 1],
                               in1=PS[pb_][0:64, 0:128],
                               op0=ALU.mult, op1=ALU.add),
                           waits=[("pe", pe_b2)], inc=0)
                    t.emit("dv", lambda e, h=h, tau=tau, jp=jp, j=j: e.tensor_mul(
                        out=YG[j * 64:j * 64 + 64,
                               (h // 2) * TB + tau * CH:(h // 2) * TB + tau * CH + CH],
                        in0=TMP[j * 64:j * 64 + 64, jp * 128:jp * 128 + 128],
                        in1=z_sl(h, tau)),
                        waits=[("ac", z_ac)] if z_ac else [], inc=0)
                    dep[("cd", jp)] = ("pe", pe_b2)
                t.emit("dv", lambda e, h=h, jp=jp: e.tensor_scalar(
                    out=STF[:, h * 64:(h + 1) * 64], in0=STF[:, h * 64:(h + 1) * 64],
                    scalar1=LCOL[:, jp:jp + 1], scalar2=None, op0=ALU.mult),
                    waits=[("pe", pe_b2)], inc=0)
                t.emit("dv", lambda e, h=h, pb_=pb_: e.tensor_add(
                    out=STF[:, h * 64:(h + 1) * 64], in0=STF[:, h * 64:(h + 1) * 64],
                    in1=PS[pb_][0:64, 128:192]), inc=0)
                t.emit("dv", lambda e, h=h: e.tensor_copy(
                    out=STB[:, h * 64:(h + 1) * 64], in_=STF[:, h * 64:(h + 1) * 64]),
                    inc=1)
                dv_st = t.snap("dv")
                dep[("stb", h)] = ("dv", dv_st)
                dep[("stf", h)] = ("dv", dv_st)
                ps_dep[pb_] = ("dv", dv_st)

            for i in range(34):
                if 1 <= i <= 32:
                    mid(i - 1)
                if i < 32:
                    front(i)
                if i >= 2:
                    back(i - 2)
            dep[("ncf", tau)] = ("pe", t.snap("pe"))

        def yln_block():
            ydep = t.snap("dv")
            for cc in range(16):
                ygc = YG[:, cc * TB:(cc + 1) * TB]
                y2s = X2[:, (cc % 2) * TB:(cc % 2 + 1) * TB]
                t.emit("ac", lambda e, ygc=ygc, y2s=y2s: e.activation(
                    out=y2s, in_=ygc, func=AF.Square),
                    waits=[("dv", ydep)]
                          + ([dep.get(("x2", cc % 2))] if dep.get(("x2", cc % 2)) else []),
                    inc=1)
                acy = t.snap("ac")
                t.emit("pe", lambda e, ygc=ygc, cc=cc: e.matmul(
                    out=PS[2][0:1, :], lhsT=CONB, rhs=ygc,
                    start=(cc == 0), stop=(cc == 15)),
                    waits=[("dv", ydep)] + ([ps_dep[2]] if cc == 0 and ps_dep[2] else []),
                    inc=0)
                t.emit("pe", lambda e, y2s=y2s, cc=cc: e.matmul(
                    out=PS[3][0:1, :], lhsT=CONB, rhs=y2s,
                    start=(cc == 0), stop=(cc == 15)),
                    waits=[("ac", acy)] + ([ps_dep[3]] if cc == 0 and ps_dep[3] else []),
                    inc=1)
                dep[("x2", cc % 2)] = ("pe", t.snap("pe"))
            pe_stats = t.snap("pe")
            ac_r1, dv_r0, r1, r0n = rows_pipeline(pe_stats)
            t.emit("pe", lambda e, r1=r1: e.matmul(out=PS[2], lhsT=CONR, rhs=r1,
                                                   start=True, stop=True),
                   waits=[("ac", ac_r1)], inc=0)
            t.emit("pe", lambda e, r0n=r0n: e.matmul(out=PS[3], lhsT=CONR, rhs=r0n,
                                                     start=True, stop=True),
                   waits=[("dv", dv_r0)], inc=1)
            pe_bc = t.snap("pe")
            for cc in range(16):
                ygc = YG[:, cc * TB:(cc + 1) * TB]
                gcol = CGAM[:, cc:cc + 1]
                t.emit("dv", lambda e, ygc=ygc, gcol=gcol: e.scalar_tensor_tensor(
                    out=GS2, in0=ygc, scalar=gcol, in1=PS[2], op0=ALU.mult, op1=ALU.mult),
                    waits=[("pe", pe_bc)] if cc == 0 else [], inc=0)
                t.emit("dv", lambda e, ygc=ygc, gcol=gcol: e.scalar_tensor_tensor(
                    out=ygc, in0=PS[3], scalar=gcol, in1=GS2, op0=ALU.mult, op1=ALU.add),
                    inc=0)
                t.emit("dv", lambda e, ygc=ygc, cc=cc: e.tensor_scalar(
                    out=ygc, in0=ygc, scalar1=CBET[:, cc:cc + 1], scalar2=None, op0=ALU.add),
                    inc=(1 if cc == 15 else 0))
            dv_yn = t.snap("dv")
            ps_dep[2] = ("dv", dv_yn)
            ps_dep[3] = ("dv", dv_yn)
            return dv_yn

        def wout_block(dv_yn):
            for oc in range(16):
                slot = oc % 4
                bank = oc % 2
                gp_l = t.emit("gp", lambda e, oc=oc, slot=slot: e.dma_start(
                    out=A2[:, slot * 2048:(slot + 1) * 2048].rearrange("p (k c) -> p k c", k=NK),
                    in_=wost[oc]),
                    waits=[a2_dep[slot]] if a2_dep[slot] else [], incs=[(f"gp{slot}", 16)])
                for k in range(NK):
                    t.emit("pe", lambda e, k=k, slot=slot, bank=bank: e.matmul(
                        out=PS[bank],
                        lhsT=A2[:, slot * 2048 + k * 128:slot * 2048 + (k + 1) * 128],
                        rhs=YG[:, k * TB:(k + 1) * TB],
                        start=(k == 0), stop=(k == NK - 1)),
                        waits=([(f"gp{slot}", gp_l), ("dv", dv_yn)]
                               + ([ps_dep[bank]] if ps_dep[bank] else [])) if k == 0 else [],
                        inc=(1 if k == NK - 1 else 0))
                pe_mm = t.snap("pe")
                a2_dep[slot] = ("pe", pe_mm)
                t.emit("ac", lambda e, oc=oc, bank=bank: e.activation(
                    out=SS[:, oc * TB:(oc + 1) * TB], in_=PS[bank], func=AF.Copy),
                    waits=[("pe", pe_mm)] + ([ss_guard[0]] if oc == 0 and ss_guard[0] else []),
                    inc=1)
                ps_dep[bank] = ("ac", t.snap("ac"))
            yg_guard[0] = ("pe", t.snap("pe"))

        def ffn_block(xn_dep):
            a1_free = [("pe", t.snap("pe")), ("dv", t.snap("dv")), ("ac", t.snap("ac")),
                       ("dv", xn_dep)]
            dv_fuse = 0
            for ic in range(64):
                sg_ = ic % 2
                su_ = 2 + ic % 2
                gbank = ic % 2
                ubank = 2 + ic % 2
                gp_g = t.emit("gp", lambda e, ic=ic, sg_=sg_: e.dma_start(
                    out=A2[:, sg_ * 2048:(sg_ + 1) * 2048].rearrange("p (k c) -> p k c", k=NK),
                    in_=wgst[ic]),
                    waits=[a2_dep[sg_]] if a2_dep[sg_] else [], incs=[(f"gp{sg_}", 16)])
                gp_u = t.emit("gp", lambda e, ic=ic, su_=su_: e.dma_start(
                    out=A2[:, su_ * 2048:(su_ + 1) * 2048].rearrange("p (k c) -> p k c", k=NK),
                    in_=wust[ic]),
                    waits=[a2_dep[su_]] if a2_dep[su_] else [], incs=[(f"gp{su_}", 16)])
                for k in range(NK):
                    t.emit("pe", lambda e, k=k, sg_=sg_, gbank=gbank: e.matmul(
                        out=PS[gbank],
                        lhsT=A2[:, sg_ * 2048 + k * 128:sg_ * 2048 + (k + 1) * 128],
                        rhs=XN[:, k * TB:(k + 1) * TB],
                        start=(k == 0), stop=(k == NK - 1)),
                        waits=([(f"gp{sg_}", gp_g)] + ([ps_dep[gbank]] if ps_dep[gbank] else [])
                               + (a1_free if ic == 0 else [])) if k == 0 else [],
                        inc=(1 if k == NK - 1 else 0))
                pe_g = t.snap("pe")
                for k in range(NK):
                    t.emit("pe", lambda e, k=k, su_=su_, ubank=ubank: e.matmul(
                        out=PS[ubank],
                        lhsT=A2[:, su_ * 2048 + k * 128:su_ * 2048 + (k + 1) * 128],
                        rhs=XN[:, k * TB:(k + 1) * TB],
                        start=(k == 0), stop=(k == NK - 1)),
                        waits=([(f"gp{su_}", gp_u)]
                               + ([ps_dep[ubank]] if ps_dep[ubank] else [])) if k == 0 else [],
                        inc=(1 if k == NK - 1 else 0))
                pe_u = t.snap("pe")
                a2_dep[sg_] = ("pe", pe_g)
                a2_dep[su_] = ("pe", pe_u)
                t.emit("ac", lambda e, ic=ic, gbank=gbank: e.activation(
                    out=GS[:, (ic % 2) * 512:(ic % 2) * 512 + 512], in_=PS[gbank],
                    func=AF.Silu, scale=CSG[:, ic:ic + 1]),
                    waits=[("pe", pe_g)]
                          + ([dep.get(("gs", ic % 2))] if dep.get(("gs", ic % 2)) else []),
                    inc=1)
                ac_g = t.snap("ac")
                t.emit("dv", lambda e, ic=ic, ubank=ubank: e.scalar_tensor_tensor(
                    out=A1[:, ic * TB:(ic + 1) * TB], in0=PS[ubank],
                    scalar=CSU[:, ic:ic + 1],
                    in1=GS[:, (ic % 2) * 512:(ic % 2) * 512 + 512],
                    op0=ALU.mult, op1=ALU.mult),
                    waits=[("pe", pe_u), ("ac", ac_g)], inc=1)
                dv_fuse = t.snap("dv")
                dep[("gs", ic % 2)] = ("dv", dv_fuse)
                ps_dep[gbank] = ("ac", ac_g)
                ps_dep[ubank] = ("dv", dv_fuse)
            return dv_fuse

        def f2_block(blk, dv_fuse):
            for oc in range(16):
                bank = oc % 2
                gp_q = []
                for hhalf in range(2):
                    slot = 2 * hhalf
                    wl = ([a2_dep[slot]] if a2_dep[slot] else []) +                          ([a2_dep[slot + 1]] if a2_dep[slot + 1] else [])
                    gp_l = t.emit("gp", lambda e, oc=oc, hhalf=hhalf, slot=slot: e.dma_start(
                        out=A2[:, slot * 2048:(slot + 2) * 2048].rearrange(
                            "p (q k c) -> p q k c", q=2, k=NK),
                        in_=wdst[oc, 2 * hhalf:2 * hhalf + 2].rearrange(
                            "q p k c -> p q k c")),
                        waits=wl, incs=[(f"gp{slot}", 16)])
                    gp_q += [gp_l, gp_l]
                xr_dep = xr_guard[oc % 2]
                sy_x = t.emit("sy", lambda e, oc=oc, blk=blk: e.dma_start(
                    out=XR[:, (oc % 2) * 512:(oc % 2) * 512 + 512], in_=xst[blk, oc]),
                    waits=[xr_dep] if xr_dep else [], incs=[(f"syx{oc % 2}", 16)])
                for k in range(64):
                    q = k // NK
                    kk = k % NK
                    t.emit("pe", lambda e, k=k, q=q, kk=kk, bank=bank: e.matmul(
                        out=PS[bank],
                        lhsT=A2[:, q * 2048 + kk * 128:q * 2048 + (kk + 1) * 128],
                        rhs=A1[:, k * TB:(k + 1) * TB],
                        start=(k == 0), stop=(k == 63)),
                        waits=([(f"gp{2 * (q // 2)}", gp_q[q]), ("dv", dv_fuse)]
                               + ([ps_dep[bank]] if ps_dep[bank] else []))
                              if kk == 0 else [],
                        inc=(1 if k == 63 else 0))
                pe_mm = t.snap("pe")
                for q in range(4):
                    a2_dep[q] = ("pe", pe_mm)
                t.emit("ac", lambda e, oc=oc, bank=bank: e.activation(
                    out=GS[:, (oc % 2) * 512:(oc % 2) * 512 + 512], in_=PS[bank],
                    func=AF.Silu, scale=CSD[:, oc:oc + 1]),
                    waits=[("pe", pe_mm)]
                          + ([dep.get(("gs", oc % 2))] if dep.get(("gs", oc % 2)) else []),
                    inc=1)
                ac_g = t.snap("ac")
                ps_dep[bank] = ("ac", ac_g)
                sl = oc % 4
                a3t = A3[:, sl * 512:(sl + 1) * 512]
                t.emit("dv", lambda e, oc=oc: e.tensor_mul(
                    out=GS2, in0=SS[:, oc * TB:(oc + 1) * TB],
                    in1=GS[:, (oc % 2) * 512:(oc % 2) * 512 + 512]),
                    waits=[("ac", ac_g), (f"syx{oc % 2}", sy_x)], inc=0)
                t.emit("dv", lambda e, a3t=a3t, oc=oc: e.tensor_add(
                    out=a3t, in0=GS2, in1=XR[:, (oc % 2) * 512:(oc % 2) * 512 + 512]),
                    waits=a3_guard[sl], inc=1)
                dv_o = t.snap("dv")
                dep[("gs", oc % 2)] = ("dv", dv_o)
                xr_guard[oc % 2] = ("dv", dv_o)
                if oc % 4 == 3:
                    g = oc // 4
                    sy_st = t.emit("sy", lambda e, blk=blk, g=g: e.dma_start(
                        out=outst[blk, 4 * g:4 * g + 4].rearrange("a p t -> p a t"),
                        in_=A3.rearrange("p (a t) -> p a t", a=4)),
                        waits=[("dv", dv_o)], incs=[("syst", 16)])
                    for sl2 in range(4):
                        a3_guard[sl2] = [("syst", sy_st)]
            ss_guard[0] = ("dv", t.snap("dv"))

        pend = None
        for pblk in range(NBLK):
            dv_norm = ln_stats(pend) if pend is not None else ln_block(xpst, pblk)
            cols = list(range(0, 16)) + list(range(32, 48)) + [64]
            for ci, cc in enumerate(cols):
                kind = "x" if cc < 16 else ("b" if cc < 48 else "a")
                proj_strip(cc, ci % 4, kind, dv_norm)
            xn_guard[0] = ("pe", t.snap("pe"))
            pend = ln_load(xpst, pblk + 1) if pblk + 1 < NBLK else ln_load(xst, 0)
            proj_dv = t.snap("dv")
            for tau in range(4):
                ssd_chunk(tau, summary_only=True, proj_dv=proj_dv)

        if DEBUG:
            t.emit("sy", lambda e: e.dma_start(out=dstf, in_=STF),
                   waits=[("dv", t.snap("dv"))], incs=[("syst", 16)])

        pending_ln = [pend]

        for blk in range(NBLK):
            if pending_ln[0] is None:
                dv_norm = ln_block(xst, blk)
            else:
                dv_norm = ln_stats(pending_ln[0])
                pending_ln[0] = None
            cols = (list(range(0, 16)) + list(range(32, 64)) + [64]
                    + list(range(16, 32)))
            for ci, cc in enumerate(cols):
                kind = ("x" if cc < 16 else "z" if cc < 32 else "b" if cc < 48
                        else "c" if cc < 64 else "a")
                proj_strip(cc, ci % 4, kind, dv_norm)
                if cc == 64:
                    la_ac_early = t.snap("ac")
                    proj_dv_early = t.snap("dv")
            proj_dv = proj_dv_early
            if DEBUG and blk == 0:
                t.emit("sy", lambda e: e.dma_start(out=dxn, in_=XN),
                       waits=[("dv", dv_norm)], incs=[("syst", 16)])
                t.emit("sy", lambda e: e.dma_start(out=dlap, in_=LAP),
                       waits=[("ac", t.snap("ac")), ("dv", proj_dv)], incs=[("syst", 16)])
                t.emit("sy", lambda e: e.dma_start(out=da1, in_=A1),
                       waits=[("ac", t.snap("ac")), ("dv", proj_dv)], incs=[("syst", 16)])
            for tau in range(4):
                ssd_chunk(tau, summary_only=False, proj_dv=proj_dv)
            if DEBUG and blk == 0:
                t.emit("sy", lambda e: e.dma_start(out=dncf, in_=NCF),
                       waits=[("dv", t.snap("dv"))], incs=[("syst", 16)])
                t.emit("sy", lambda e: e.dma_start(out=dnct, in_=NCT),
                       waits=[("ac", t.snap("ac"))], incs=[("syst", 16)])
                t.emit("sy", lambda e: e.dma_start(out=dyg, in_=YG),
                       waits=[("dv", t.snap("dv"))], incs=[("syst", 16)])
            dv_yn = yln_block()
            wout_block(dv_yn)
            if DEBUG and blk == 0:
                t.emit("sy", lambda e: e.dma_start(out=dyn, in_=YG),
                       waits=[("dv", dv_yn)], incs=[("syst", 16)])
                t.emit("sy", lambda e: e.dma_start(out=dss, in_=SS),
                       waits=[("ac", t.snap("ac"))], incs=[("syst", 16)])
            dv_fuse = ffn_block(dv_norm)
            if DEBUG and blk == 0:
                t.emit("sy", lambda e: e.dma_start(out=dfuse, in_=A1[:, 0:4096]),
                       waits=[("dv", dv_fuse)], incs=[("syst", 16)])
            xn_guard[0] = ("pe", t.snap("pe"))
            if blk + 1 < NBLK:
                pending_ln[0] = ln_load(xst, blk + 1)
            f2_block(blk, dv_fuse)

        t.emit("sy", lambda e: e.wait_ge(SEM["syst"], t.cnt["syst"]))

        def replay(ek):
            def run(eng):
                hw = {}
                for fn, waits, incs in t.tape[ek]:
                    for (sk, v) in waits:
                        if v > hw.get(sk, 0):
                            eng.wait_ge(SEM[sk], v)
                            hw[sk] = v
                    ins = fn(eng)
                    if incs and ins is not None:
                        for sk, amt in incs:
                            ins = ins.then_inc(SEM[sk], amt)
            return run

        block.sync(replay("sy"))
        block.gpsimd(replay("gp"))
        block.tensor(replay("pe"))
        block.vector(replay("dv"))
        block.scalar(replay("ac"))

    return nc


_NC_CACHE = {}


def _stage_shared(W_in, W_out, gamma, beta, D, w_gate, w_up, w_down, s_gate, s_up, s_down):
    W_in = np.asarray(W_in, np.float32)
    W_in_pad = np.zeros((65 * 128, DM), np.float32)
    W_in_pad[:8224] = W_in
    wst = np.ascontiguousarray(W_in_pad.reshape(65, 128, NK, 128).transpose(0, 3, 2, 1))
    W_out = np.asarray(W_out, np.float32)
    wost = np.ascontiguousarray(W_out.reshape(16, 128, NK, 128).transpose(0, 3, 2, 1))
    wg = np.asarray(w_gate)
    wgst = np.ascontiguousarray(wg.reshape(NK, 128, 64, 128).transpose(2, 1, 0, 3))
    wu = np.asarray(w_up)
    wust = np.ascontiguousarray(wu.reshape(NK, 128, 64, 128).transpose(2, 1, 0, 3))
    wd = np.asarray(w_down)
    wdst = np.ascontiguousarray(
        wd.reshape(4, NK, 128, 16, 128).transpose(3, 0, 2, 1, 4))
    gamv = np.ascontiguousarray(np.asarray(gamma, np.float32).reshape(16, 128).T)
    betv = np.ascontiguousarray(np.asarray(beta, np.float32).reshape(16, 128).T)
    sgv = np.ascontiguousarray(np.asarray(s_gate, np.float32).reshape(64, 128).T)
    suv = np.ascontiguousarray(np.asarray(s_up, np.float32).reshape(64, 128).T)
    sdv = np.ascontiguousarray(np.asarray(s_down, np.float32).reshape(16, 128).T)
    d2 = np.ascontiguousarray(np.tile(np.asarray(D, np.float32).T, (2, 1)))
    nege32 = np.zeros((32, 32 * 128), np.float32)
    for h in range(32):
        nege32[h, h * 128:(h + 1) * 128] = -1.0
    identb = np.eye(128).astype(ml_dtypes.bfloat16)
    identf = np.eye(32, dtype=np.float32)
    maskt = np.tril(np.full((128, 128), np.float32(MASKVAL), dtype=np.float32), k=-1
                    ).astype(ml_dtypes.bfloat16)
    onesb = np.ones((128, 1), ml_dtypes.bfloat16)
    onesr = np.ones((1, 128), np.float32)
    return dict(wst=wst, wost=wost, wgst=wgst, wust=wust, wdst=wdst,
                gam=gamv, bet=betv, sgm=sgv, sum_=suv, sdm=sdv, d2m=d2,
                nege32=nege32, identb=identb, identf=identf,
                maskt=maskt, onesb=onesb, onesr=onesr,
                fconst=np.tile(np.array([[EPS, 1.0 / DM, -0.5, -1.0, 0.0, 1.0, 0.0,
                                          0.0]], np.float32), (128, 1)))


def _stage_x(xslice):
    xT = np.ascontiguousarray(np.asarray(xslice, np.float32).T)
    return np.ascontiguousarray(xT.reshape(NK, 128, NBLK, TB).transpose(2, 0, 1, 3))


def kernel(L0_attn_x, L0_attn_W_in, L0_attn_W_out, L0_attn_gamma, L0_attn_beta,
           L0_attn_D, L0_ffn_w_gate, L0_ffn_w_up, L0_ffn_w_down,
           L0_ffn_s_gate, L0_ffn_s_up, L0_ffn_s_down):
    x = np.asarray(L0_attn_x, np.float32)
    shared = _stage_shared(L0_attn_W_in, L0_attn_W_out, L0_attn_gamma, L0_attn_beta,
                           L0_attn_D, L0_ffn_w_gate, L0_ffn_w_up, L0_ffn_w_down,
                           L0_ffn_s_gate, L0_ffn_s_up, L0_ffn_s_down)
    zeros_prev = np.zeros((NBLK, NK, 128, TB), np.float32)
    in_maps = []
    for c in range(NCORES):
        b, half = c // 2, c % 2
        xs = _stage_x(x[b, half * T:(half + 1) * T, :])
        xp = _stage_x(x[b, 0:T, :]) if half == 1 else zeros_prev
        in_maps.append(dict(xst=xs, xpst=xp, **shared))

    if "nc" not in _NC_CACHE:
        _NC_CACHE["nc"] = build_program()
    nc = _NC_CACHE["nc"]
    res = run_bass_kernel_spmd(nc, in_maps, core_ids=list(range(NCORES)))

    out = np.empty((4, 4096, DM), np.float32)
    for c in range(NCORES):
        b, half = c // 2, c % 2
        o = res.results[c]["outst"]
        ot = np.ascontiguousarray(o.transpose(0, 3, 1, 2)).reshape(T, DM)
        out[b, half * T:(half + 1) * T, :] = ot
    return out




# version 20
# speedup vs baseline: 1.0402x; 1.0402x over previous
import contextlib
import numpy as np
import ml_dtypes
import concourse.bass as bass
import concourse.mybir as mybir
from concourse.bass_utils import run_bass_kernel_spmd

F32 = mybir.dt.float32
BF16 = mybir.dt.bfloat16
I8 = mybir.dt.int8
AF = mybir.ActivationFunctionType
ALU = mybir.AluOpType

NCORES = 8
T = 2048
DM = 2048
NBLK = 4
TB = 512
NK = 16
CH = 128
EPS = 1e-5
MASKVAL = -1e9

ENGS = ("sy", "gp", "pe", "dv", "ac")
SEMS = ("pe", "dv", "ac", "gpc", "gpx",
        "syc", "sya0", "sya1", "sya2", "sya3", "syx0", "syx1", "syst",
        "gp0", "gp1", "gp2", "gp3")
DEBUG = False


class Tape:
    def __init__(self):
        self.tape = {k: [] for k in ENGS}
        self.cnt = {k: 0 for k in SEMS}

    def emit(self, ek, fn, waits=(), inc=0, incs=None):
        w = [(sk, v) for (sk, v) in waits if v and v > 0]
        if incs is None:
            incs = [(ek, inc)] if inc else []
        self.tape[ek].append((fn, w, incs))
        ret = 0
        for sk, amt in incs:
            self.cnt[sk] += amt
            ret = self.cnt[sk]
        return ret

    def snap(self, ek):
        return self.cnt[ek]


def build_program():
    nc = bass.Bass("TRN2", target_bir_lowering=False, debug=False)

    xst = nc.dram_tensor("xst", [NBLK, NK, 128, TB], F32, kind="ExternalInput").ap()
    xpst = nc.dram_tensor("xpst", [NBLK, NK, 128, TB], F32, kind="ExternalInput").ap()
    wst = nc.dram_tensor("wst", [65, 128, NK, 128], F32, kind="ExternalInput").ap()
    wost = nc.dram_tensor("wost", [16, 128, NK, 128], F32, kind="ExternalInput").ap()
    wgst = nc.dram_tensor("wgst", [64, 128, NK, 128], I8, kind="ExternalInput").ap()
    wust = nc.dram_tensor("wust", [64, 128, NK, 128], I8, kind="ExternalInput").ap()
    wdst = nc.dram_tensor("wdst", [16, 4, 128, NK, 128], I8, kind="ExternalInput").ap()
    gam = nc.dram_tensor("gam", [128, 16], F32, kind="ExternalInput").ap()
    bet = nc.dram_tensor("bet", [128, 16], F32, kind="ExternalInput").ap()
    sgm = nc.dram_tensor("sgm", [128, 64], F32, kind="ExternalInput").ap()
    sum_ = nc.dram_tensor("sum_", [128, 64], F32, kind="ExternalInput").ap()
    sdm = nc.dram_tensor("sdm", [128, 16], F32, kind="ExternalInput").ap()
    d2m = nc.dram_tensor("d2m", [128, 32], F32, kind="ExternalInput").ap()
    nege32 = nc.dram_tensor("nege32", [32, 32 * 128], F32, kind="ExternalInput").ap()
    identb = nc.dram_tensor("identb", [128, 128], BF16, kind="ExternalInput").ap()
    identf = nc.dram_tensor("identf", [32, 32], F32, kind="ExternalInput").ap()
    maskt = nc.dram_tensor("maskt", [128, 128], BF16, kind="ExternalInput").ap()
    onesb = nc.dram_tensor("onesb", [128, 1], BF16, kind="ExternalInput").ap()
    onesr = nc.dram_tensor("onesr", [1, 128], F32, kind="ExternalInput").ap()
    fconst = nc.dram_tensor("fconst", [128, 8], F32, kind="ExternalInput").ap()
    outst = nc.dram_tensor("outst", [NBLK, 16, 128, TB], F32, kind="ExternalOutput").ap()
    if DEBUG:
        dxn = nc.dram_tensor("dxn", [128, 8192], BF16, kind="ExternalOutput").ap()
        dlap = nc.dram_tensor("dlap", [32, 512], F32, kind="ExternalOutput").ap()
        dncf = nc.dram_tensor("dncf", [32, 512], F32, kind="ExternalOutput").ap()
        dnct = nc.dram_tensor("dnct", [128, 128], F32, kind="ExternalOutput").ap()
        dyg = nc.dram_tensor("dyg", [128, 8192], BF16, kind="ExternalOutput").ap()
        dyn = nc.dram_tensor("dyn", [128, 8192], BF16, kind="ExternalOutput").ap()
        dss = nc.dram_tensor("dss", [128, 8192], BF16, kind="ExternalOutput").ap()
        dstf = nc.dram_tensor("dstf", [64, 2048], F32, kind="ExternalOutput").ap()
        dfuse = nc.dram_tensor("dfuse", [128, 4096], BF16, kind="ExternalOutput").ap()
        da1 = nc.dram_tensor("da1", [128, 32768], BF16, kind="ExternalOutput").ap()

    with contextlib.ExitStack() as stack:
        sb = lambda n, s, d: stack.enter_context(nc.sbuf_tensor(n, s, d)).ap()

        A1 = sb("A1", [128, 32768], BF16)
        A2 = sb("A2", [128, 8192], BF16)
        A3 = sb("A3", [128, 2048], F32)
        XN = sb("XN", [128, 8192], BF16)
        YG = sb("YG", [128, 8192], BF16)
        SS = sb("SS", [128, 8192], BF16)
        GS = sb("GS", [128, 1024], F32)
        GS2 = sb("GS2", [128, 512], F32)
        XR = sb("XR", [128, 1024], F32)
        X2 = sb("X2", [128, 1024], BF16)
        MT = sb("MT", [128, 512], F32)
        STt = sb("STt", [128, 512], BF16)
        DE = sb("DE", [128, 512], F32)
        CDt = sb("CDt", [64, 512], BF16)
        BW = sb("BW", [128, 256], BF16)
        XT = sb("XT", [128, 512], BF16)
        TMP = sb("TMP", [128, 512], F32)
        NCF = sb("NCF", [32, 512], F32)
        LAP = sb("LAP", [32, 512], F32)
        NCT = sb("NCT", [128, 128], F32)
        WCOL = sb("WCOL", [128, 4], F32)
        LCOL = sb("LCOL", [64, 4], F32)
        ONES32 = sb("ONES32", [32, 128], F32)
        ROWS = sb("ROWS", [1, 3584], F32)
        STF = sb("STF", [64, 2048], F32)
        STB = sb("STB", [64, 2048], BF16)
        CN32 = sb("CN32", [32, 32 * 128], F32)
        CIDB = sb("CIDB", [128, 128], BF16)
        CIDF = sb("CIDF", [32, 32], F32)
        CMSK = sb("CMSK", [128, 128], BF16)
        CONB = sb("CONB", [128, 1], BF16)
        CONR = sb("CONR", [1, 128], F32)
        CGAM = sb("CGAM", [128, 16], F32)
        CBET = sb("CBET", [128, 16], F32)
        CSG = sb("CSG", [128, 64], F32)
        CSU = sb("CSU", [128, 64], F32)
        CSD = sb("CSD", [128, 16], F32)
        CD2 = sb("CD2", [128, 32], F32)
        CFC = sb("CFC", [128, 8], F32)
        for i, v in enumerate([EPS, 1.0 / DM, -0.5, -1.0, 0.0, 1.0]):
            nc.const_aps.register(F32, v, CFC[:, i:i + 1])

        PS = [stack.enter_context(nc.psum_tensor(f"ps{i}", [128, 512], F32)).ap()
              for i in range(8)]
        PSB = {i: PS[i].bitcast(BF16) for i in range(8)}

        SEM = {k: stack.enter_context(nc.semaphore(f"sem_{k}")) for k in SEMS}
        block = stack.enter_context(nc.Block())

        t = Tape()
        dep = {}
        ps_dep = {i: None for i in range(8)}
        a2_dep = {i: None for i in range(4)}
        xn_guard = [None]
        a3_guard = {i: [] for i in range(4)}
        xr_guard = {0: None, 1: None}
        ss_guard = [None]
        yg_guard = [None]

        for (dst, src) in [(CN32, nege32), (CIDB, identb), (CIDF, identf),
                           (CMSK, maskt), (CONB, onesb), (CONR, onesr),
                           (CGAM, gam), (CBET, bet), (CSG, sgm), (CSU, sum_),
                           (CSD, sdm), (CD2, d2m), (CFC, fconst)]:
            t.emit("sy", lambda e, d=dst, s=src: e.dma_start(out=d, in_=s),
                   incs=[("syc", 16)])
        consts_sy = t.snap("syc")
        for ek in ("pe", "dv", "ac", "gp"):
            t.emit(ek, lambda e: None, waits=[("syc", consts_sy)])
        t.emit("dv", lambda e: e.memset(ONES32, 1.0), inc=1)
        t.emit("dv", lambda e: e.memset(STF, 0.0), inc=1)
        t.emit("dv", lambda e: e.memset(STB, 0.0), inc=1)

        def rows_pipeline(wait_pe):
            mu = ROWS[:, 0:512]; e2 = ROWS[:, 512:1024]; mu2 = ROWS[:, 1024:1536]
            var = ROWS[:, 1536:2048]; lnv = ROWS[:, 2048:2560]
            r1 = ROWS[:, 2560:3072]; r0n = ROWS[:, 3072:3584]
            t.emit("ac", lambda e: e.activation(out=mu, in_=PS[2][0:1, :], func=AF.Copy,
                                                scale=1.0 / DM), waits=[("pe", wait_pe)], inc=1)
            t.emit("ac", lambda e: e.activation(out=e2, in_=PS[3][0:1, :], func=AF.Copy,
                                                scale=1.0 / DM), inc=1)
            t.emit("ac", lambda e: e.activation(out=mu2, in_=mu, func=AF.Square), inc=1)
            ac_r = t.snap("ac")
            t.emit("dv", lambda e: e.tensor_sub(out=var, in0=e2, in1=mu2),
                   waits=[("ac", ac_r)], inc=1)
            dv_r = t.snap("dv")
            t.emit("ac", lambda e: e.activation(out=lnv, in_=var, func=AF.Ln, bias=EPS),
                   waits=[("dv", dv_r)], inc=1)
            t.emit("ac", lambda e: e.activation(out=r1, in_=lnv, func=AF.Exp, scale=-0.5),
                   inc=1)
            ac_r1 = t.snap("ac")
            t.emit("dv", lambda e: e.scalar_tensor_tensor(
                out=r0n, in0=mu, scalar=-1.0, in1=r1, op0=ALU.mult, op1=ALU.mult),
                waits=[("ac", ac_r1)], inc=1)
            dv_r0 = t.snap("dv")
            return ac_r1, dv_r0, r1, r0n

        def ln_load(stage, blk):
            return t.emit("gp", lambda e, blk=blk: e.dma_start(
                out=XN.rearrange("p (k t) -> p k t", k=NK),
                in_=stage[blk].rearrange("k p t -> p k t")),
                waits=([xn_guard[0]] if xn_guard[0] else []), incs=[("gpx", 16)])

        def ln_stats(gp_x):
            for k in range(NK):
                xk = XN[:, k * TB:(k + 1) * TB]
                x2s = X2[:, (k % 2) * TB:(k % 2 + 1) * TB]
                t.emit("ac", lambda e, xk=xk, x2s=x2s: e.activation(
                    out=x2s, in_=xk, func=AF.Square),
                    waits=[("gpx", gp_x)]
                          + ([dep.get(("x2", k % 2))] if dep.get(("x2", k % 2)) else []),
                    inc=1)
                ack = t.snap("ac")
                t.emit("pe", lambda e, xk=xk, k=k: e.matmul(
                    out=PS[2][0:1, :], lhsT=CONB, rhs=xk,
                    start=(k == 0), stop=(k == NK - 1)),
                    waits=[("gpx", gp_x)] + ([ps_dep[2]] if k == 0 and ps_dep[2] else []),
                    inc=0)
                t.emit("pe", lambda e, x2s=x2s, k=k: e.matmul(
                    out=PS[3][0:1, :], lhsT=CONB, rhs=x2s,
                    start=(k == 0), stop=(k == NK - 1)),
                    waits=[("ac", ack)] + ([ps_dep[3]] if k == 0 and ps_dep[3] else []),
                    inc=1)
                dep[("x2", k % 2)] = ("pe", t.snap("pe"))
            pe_stats = t.snap("pe")
            ac_r1, dv_r0, r1, r0n = rows_pipeline(pe_stats)
            t.emit("pe", lambda e, r1=r1: e.matmul(out=PS[2], lhsT=CONR, rhs=r1,
                                                   start=True, stop=True),
                   waits=[("ac", ac_r1)], inc=0)
            t.emit("pe", lambda e, r0n=r0n: e.matmul(out=PS[3], lhsT=CONR, rhs=r0n,
                                                     start=True, stop=True),
                   waits=[("dv", dv_r0)], inc=1)
            pe_bc = t.snap("pe")
            for k in range(NK):
                xk = XN[:, k * TB:(k + 1) * TB]
                t.emit("dv", lambda e, xk=xk: e.tensor_mul(out=xk, in0=xk, in1=PS[2]),
                       waits=[("pe", pe_bc)] if k == 0 else [], inc=0)
                t.emit("dv", lambda e, xk=xk: e.tensor_add(out=xk, in0=xk, in1=PS[3]),
                       inc=(1 if k == NK - 1 else 0))
            dv_norm = t.snap("dv")
            ps_dep[2] = ("dv", dv_norm)
            ps_dep[3] = ("dv", dv_norm)
            return dv_norm

        def ln_block(stage, blk):
            return ln_stats(ln_load(stage, blk))

        def proj_strip(cc, slot, kind, xn_dep):
            bank = cc % 2
            gp_l = t.emit("gp", lambda e, cc=cc, slot=slot: e.dma_start(
                out=A2[:, slot * 2048:(slot + 1) * 2048].rearrange("p (k c) -> p k c", k=NK),
                in_=wst[cc]),
                waits=[a2_dep[slot]] if a2_dep[slot] else [], incs=[(f"gp{slot}", 16)])
            for k in range(NK):
                t.emit("pe", lambda e, k=k, slot=slot, bank=bank: e.matmul(
                    out=PS[bank],
                    lhsT=A2[:, slot * 2048 + k * 128:slot * 2048 + (k + 1) * 128],
                    rhs=XN[:, k * TB:(k + 1) * TB],
                    start=(k == 0), stop=(k == NK - 1)),
                    waits=([(f"gp{slot}", gp_l), ("dv", xn_dep)]
                           + ([ps_dep[bank]] if ps_dep[bank] else [])) if k == 0 else [],
                    inc=(1 if k == NK - 1 else 0))
            pe_mm = t.snap("pe")
            a2_dep[slot] = ("pe", pe_mm)
            if kind == "x":
                dst = A1[:, cc * TB:(cc + 1) * TB]
            elif kind == "z":
                dst = A1[:, 16384 + (cc - 16) * TB:16384 + (cc - 15) * TB]
            elif kind == "b":
                dst = A1[:, 8192 + (cc - 32) * TB:8192 + (cc - 31) * TB]
            elif kind == "c":
                dst = A1[:, 24576 + (cc - 48) * TB:24576 + (cc - 47) * TB]
            if kind in ("x", "b"):
                t.emit("dv", lambda e, dst=dst, bank=bank: e.tensor_copy(out=dst, in_=PS[bank]),
                       waits=[("pe", pe_mm)], inc=1)
                ps_dep[bank] = ("dv", t.snap("dv"))
            elif kind == "z":
                t.emit("ac", lambda e, dst=dst, bank=bank: e.activation(
                    out=dst, in_=PS[bank], func=AF.Silu), waits=[("pe", pe_mm)], inc=1)
                ps_dep[bank] = ("ac", t.snap("ac"))
            elif kind == "c":
                t.emit("ac", lambda e, dst=dst, bank=bank: e.activation(
                    out=dst, in_=PS[bank], func=AF.Copy), waits=[("pe", pe_mm)], inc=1)
                ps_dep[bank] = ("ac", t.snap("ac"))
            elif kind == "a":
                t.emit("ac", lambda e, bank=bank: e.activation(
                    out=LAP, in_=PS[bank][0:32, :], func=AF.Exp),
                    waits=[("pe", pe_mm)], inc=1)
                ps_dep[bank] = ("ac", t.snap("ac"))

        def xf_sl(h, tau):
            o = (h // 2) * TB + tau * CH
            return A1[(h % 2) * 64:(h % 2) * 64 + 64, o:o + CH]

        def b_sl(h, tau):
            o = 8192 + (h // 2) * TB + tau * CH
            return A1[(h % 2) * 64:(h % 2) * 64 + 64, o:o + CH]

        def z_sl(h, tau):
            o = 16384 + (h // 2) * TB + tau * CH
            return A1[(h % 2) * 64:(h % 2) * 64 + 64, o:o + CH]

        def c_sl(h, tau):
            o = 24576 + (h // 2) * TB + tau * CH
            return A1[(h % 2) * 64:(h % 2) * 64 + 64, o:o + CH]

        def b_pair(hp, tau):
            o = 8192 + hp * TB + tau * CH
            return A1[:, o:o + CH]

        def x_pair(hp, tau):
            o = hp * TB + tau * CH
            return A1[:, o:o + CH]

        P_H = [4, 5, 0, 1]
        T_H = [6, 7, 2, 3]

        def ssd_chunk(tau, summary_only, proj_dv, la_ac=None, z_ac=None):
            if la_ac is None:
                la_ac = t.snap("ac")
            ncf_sl = NCF[:, tau * CH:(tau + 1) * CH]
            t.emit("dv", lambda e, ncf_sl=ncf_sl, tau=tau: e.tensor_tensor_scan(
                out=ncf_sl, data0=ONES32, data1=LAP[:, tau * CH:(tau + 1) * CH],
                initial=0.0, op0=ALU.mult, op1=ALU.add),
                waits=[("ac", la_ac)] + ([dep.get(("ncf", tau))] if dep.get(("ncf", tau)) else []),
                inc=1)
            dv_sc = t.snap("dv")
            t.emit("pe", lambda e, ncf_sl=ncf_sl: e.transpose(
                out=PS[6][:, 448:480], in_=ncf_sl, identity=CIDF),
                waits=[("dv", dv_sc)]
                      + ([dep.get(("trb", 6))] if dep.get(("trb", 6)) else [])
                      + ([dep.get("nct_evac")] if dep.get("nct_evac") else []),
                inc=1)
            pe_tr = t.snap("pe")
            nct_sl = NCT[:, tau * 32:(tau + 1) * 32]
            t.emit("ac", lambda e, nct_sl=nct_sl: e.copy(out=nct_sl, in_=PS[6][:, 448:480]),
                   waits=[("pe", pe_tr)], inc=1)
            dep["nct_evac"] = ("ac", t.snap("ac"))

            ctx = {}

            def front(i):
                h = i
                hp, j = i // 2, i % 2
                jp = h % 4
                pb_ = P_H[jp]
                tb_ = T_H[hp % 4]
                pbank = PSB[tb_]
                c = {"h": h, "hp": hp, "j": j, "jp": jp, "pb_": pb_, "tb_": tb_,
                     "pbank": pbank}
                if j == 0:
                    trw = ([dep.get(("trb", tb_))] if dep.get(("trb", tb_)) else [])
                    if tb_ == 6:
                        trw = trw + [dep["nct_evac"]]
                    t.emit("pe", lambda e, hp=hp, tau=tau, pbank=pbank: e.transpose(
                        out=pbank[:, 0:128], in_=b_pair(hp, tau), identity=CIDB),
                        waits=trw + [("dv", proj_dv)], inc=0)
                    t.emit("pe", lambda e, hp=hp, tau=tau, pbank=pbank: e.transpose(
                        out=pbank[:, 128:256], in_=x_pair(hp, tau), identity=CIDB), inc=1)
                    pe_trp = t.snap("pe")
                    t.emit("ac", lambda e, hp=hp, pbank=pbank: e.copy(
                        out=XT[:, (hp % 4) * 128:(hp % 4) * 128 + 128],
                        in_=pbank[:, 128:256]),
                        waits=[("pe", pe_trp)]
                              + ([dep.get(("xt_use", hp % 4))] if dep.get(("xt_use", hp % 4))
                                 else []),
                        inc=1)
                    dep[("acxt", hp)] = t.snap("ac")
                c["ac_xt"] = dep[("acxt", hp)]
                if not summary_only:
                    t.emit("pe", lambda e, h=h, tau=tau, pb_=pb_: e.matmul(
                        out=PS[pb_][:, 192:320], lhsT=b_sl(h, tau),
                        rhs=c_sl(h, tau), start=True, stop=True),
                        waits=([ps_dep[pb_]] if ps_dep[pb_] else [])
                              + [("dv", proj_dv)], inc=0)
                t.emit("pe", lambda e, h=h, tau=tau, pb_=pb_: e.matmul(
                    out=PS[pb_][:, 320:448],
                    lhsT=CN32[:, h * 128:(h + 1) * 128],
                    rhs=NCF[:, tau * CH:(tau + 1) * CH],
                    start=True, stop=summary_only),
                    waits=[ps_dep[pb_]] if ps_dep[pb_] else [],
                    inc=1)
                pe_b1 = t.snap("pe")
                t.emit("ac", lambda e, jp=jp, pb_=pb_, tau=tau, h=h: e.activation(
                    out=WCOL[:, jp:jp + 1], in_=PS[pb_][:, 447:448], func=AF.Exp,
                    bias=NCT[:, tau * 32 + h:tau * 32 + h + 1]),
                    waits=[("pe", pe_b1)], inc=0)
                t.emit("ac", lambda e, jp=jp, pb_=pb_: e.activation(
                    out=LCOL[:, jp:jp + 1], in_=PS[pb_][0:64, 447:448], func=AF.Exp),
                    inc=(1 if summary_only else 0))
                if not summary_only:
                    t.emit("ac", lambda e, jp=jp, j=j, pb_=pb_: e.activation(
                        out=DE[j * 64:j * 64 + 64, jp * 128:jp * 128 + 128],
                        in_=PS[pb_][0:64, 320:448], func=AF.Exp), inc=1)
                c["ac_a"] = t.snap("ac")
                ctx[i] = c

            def mid(i):
                c = ctx[i]
                h, hp, j, jp = c["h"], c["hp"], c["j"], c["jp"]
                pb_, tb_, pbank = c["pb_"], c["tb_"], c["pbank"]
                if not summary_only:
                    t.emit("pe", lambda e, pb_=pb_: e.matmul(
                        out=PS[pb_][:, 320:448], lhsT=CIDB, rhs=CMSK,
                        start=False, stop=True),
                        waits=[("ac", c["ac_a"])], inc=1)
                    pe_msk = t.snap("pe")
                    t.emit("ac", lambda e, jp=jp, pb_=pb_, tau=tau, h=h: e.activation(
                        out=MT[:, jp * 128:jp * 128 + 128], in_=PS[pb_][:, 320:448],
                        func=AF.Exp, bias=NCT[:, tau * 32 + h:tau * 32 + h + 1]),
                        waits=[("pe", pe_msk)], inc=1)
                    ac_b = t.snap("ac")
                t.emit("dv", lambda e, jp=jp, j=j, pbank=pbank: e.tensor_scalar(
                    out=BW[:, jp * 64:jp * 64 + 64],
                    in0=pbank[:, j * 64:j * 64 + 64],
                    scalar1=WCOL[:, jp:jp + 1], scalar2=None, op0=ALU.mult),
                    waits=[("ac", c["ac_a"])], inc=(1 if summary_only else 0))
                if not summary_only:
                    t.emit("dv", lambda e, h=h, tau=tau, jp=jp, j=j: e.tensor_mul(
                        out=CDt[:, jp * 128:jp * 128 + 128], in0=c_sl(h, tau),
                        in1=DE[j * 64:j * 64 + 64, jp * 128:jp * 128 + 128]),
                        waits=([dep.get(("cd", jp))] if dep.get(("cd", jp)) else []),
                        inc=0)
                    t.emit("dv", lambda e, jp=jp, pb_=pb_: e.tensor_mul(
                        out=STt[:, jp * 128:jp * 128 + 128],
                        in0=PS[pb_][:, 192:320],
                        in1=MT[:, jp * 128:jp * 128 + 128]),
                        waits=[("ac", ac_b)], inc=1)
                c["dv_s"] = t.snap("dv")

            def back(i):
                c = ctx.pop(i)
                h, hp, j, jp = c["h"], c["hp"], c["j"], c["jp"]
                pb_, tb_ = c["pb_"], c["tb_"]
                dv_s, ac_xt = c["dv_s"], c["ac_xt"]
                stb_dep = dep.get(("stb", h))
                if not summary_only:
                    t.emit("pe", lambda e, jp=jp, hp=hp, j=j, pb_=pb_: e.matmul(
                        out=PS[pb_][0:64, 0:128],
                        lhsT=XT[:, (hp % 4) * 128 + j * 64:(hp % 4) * 128 + j * 64 + 64],
                        rhs=STt[:, jp * 128:jp * 128 + 128],
                        start=True, stop=False),
                        waits=[("dv", dv_s), ("ac", ac_xt)]
                              + ([stb_dep] if stb_dep else []), inc=0)
                    t.emit("pe", lambda e, h=h, jp=jp, pb_=pb_: e.matmul(
                        out=PS[pb_][0:64, 0:128],
                        lhsT=STB[:, h * 64:(h + 1) * 64],
                        rhs=CDt[:, jp * 128:jp * 128 + 128],
                        start=False, stop=True), inc=0)
                t.emit("pe", lambda e, jp=jp, hp=hp, j=j, pb_=pb_: e.matmul(
                    out=PS[pb_][0:64, 128:192],
                    lhsT=BW[:, jp * 64:jp * 64 + 64],
                    rhs=XT[:, (hp % 4) * 128 + j * 64:(hp % 4) * 128 + j * 64 + 64],
                    start=True, stop=True),
                    waits=([("dv", dv_s), ("ac", ac_xt)]
                           + ([dep.get(("stf", h))] if dep.get(("stf", h)) else []))
                          if summary_only else [],
                    inc=1)
                pe_b2 = t.snap("pe")
                if j == 1:
                    dep[("xt_use", hp % 4)] = ("pe", pe_b2)
                    dep[("trb", tb_)] = ("dv", dv_s)
                if not summary_only:
                    t.emit("dv", lambda e, h=h, tau=tau, jp=jp, j=j, pb_=pb_:
                           e.scalar_tensor_tensor(
                               out=TMP[j * 64:j * 64 + 64, jp * 128:jp * 128 + 128],
                               in0=xf_sl(h, tau),
                               scalar=CD2[j * 64:j * 64 + 64, h:h + 1],
                               in1=PS[pb_][0:64, 0:128],
                               op0=ALU.mult, op1=ALU.add),
                           waits=[("pe", pe_b2)], inc=0)
                    t.emit("dv", lambda e, h=h, tau=tau, jp=jp, j=j: e.tensor_mul(
                        out=YG[j * 64:j * 64 + 64,
                               (h // 2) * TB + tau * CH:(h // 2) * TB + tau * CH + CH],
                        in0=TMP[j * 64:j * 64 + 64, jp * 128:jp * 128 + 128],
                        in1=z_sl(h, tau)),
                        waits=[("ac", z_ac)] if z_ac else [], inc=0)
                    dep[("cd", jp)] = ("pe", pe_b2)
                t.emit("dv", lambda e, h=h, jp=jp: e.tensor_scalar(
                    out=STF[:, h * 64:(h + 1) * 64], in0=STF[:, h * 64:(h + 1) * 64],
                    scalar1=LCOL[:, jp:jp + 1], scalar2=None, op0=ALU.mult),
                    waits=[("pe", pe_b2)], inc=0)
                t.emit("dv", lambda e, h=h, pb_=pb_: e.tensor_add(
                    out=STF[:, h * 64:(h + 1) * 64], in0=STF[:, h * 64:(h + 1) * 64],
                    in1=PS[pb_][0:64, 128:192]), inc=0)
                t.emit("dv", lambda e, h=h: e.tensor_copy(
                    out=STB[:, h * 64:(h + 1) * 64], in_=STF[:, h * 64:(h + 1) * 64]),
                    inc=1)
                dv_st = t.snap("dv")
                dep[("stb", h)] = ("dv", dv_st)
                dep[("stf", h)] = ("dv", dv_st)
                ps_dep[pb_] = ("dv", dv_st)

            for i in range(35):
                if 1 <= i <= 32:
                    mid(i - 1)
                if i < 32:
                    front(i)
                if i >= 3:
                    back(i - 3)
            dep[("ncf", tau)] = ("pe", t.snap("pe"))

        def yln_block():
            ydep = t.snap("dv")
            for cc in range(16):
                ygc = YG[:, cc * TB:(cc + 1) * TB]
                y2s = X2[:, (cc % 2) * TB:(cc % 2 + 1) * TB]
                t.emit("ac", lambda e, ygc=ygc, y2s=y2s: e.activation(
                    out=y2s, in_=ygc, func=AF.Square),
                    waits=[("dv", ydep)]
                          + ([dep.get(("x2", cc % 2))] if dep.get(("x2", cc % 2)) else []),
                    inc=1)
                acy = t.snap("ac")
                t.emit("pe", lambda e, ygc=ygc, cc=cc: e.matmul(
                    out=PS[2][0:1, :], lhsT=CONB, rhs=ygc,
                    start=(cc == 0), stop=(cc == 15)),
                    waits=[("dv", ydep)] + ([ps_dep[2]] if cc == 0 and ps_dep[2] else []),
                    inc=0)
                t.emit("pe", lambda e, y2s=y2s, cc=cc: e.matmul(
                    out=PS[3][0:1, :], lhsT=CONB, rhs=y2s,
                    start=(cc == 0), stop=(cc == 15)),
                    waits=[("ac", acy)] + ([ps_dep[3]] if cc == 0 and ps_dep[3] else []),
                    inc=1)
                dep[("x2", cc % 2)] = ("pe", t.snap("pe"))
            pe_stats = t.snap("pe")
            ac_r1, dv_r0, r1, r0n = rows_pipeline(pe_stats)
            t.emit("pe", lambda e, r1=r1: e.matmul(out=PS[2], lhsT=CONR, rhs=r1,
                                                   start=True, stop=True),
                   waits=[("ac", ac_r1)], inc=0)
            t.emit("pe", lambda e, r0n=r0n: e.matmul(out=PS[3], lhsT=CONR, rhs=r0n,
                                                     start=True, stop=True),
                   waits=[("dv", dv_r0)], inc=1)
            pe_bc = t.snap("pe")
            for cc in range(16):
                ygc = YG[:, cc * TB:(cc + 1) * TB]
                gcol = CGAM[:, cc:cc + 1]
                t.emit("dv", lambda e, ygc=ygc, gcol=gcol: e.scalar_tensor_tensor(
                    out=GS2, in0=ygc, scalar=gcol, in1=PS[2], op0=ALU.mult, op1=ALU.mult),
                    waits=[("pe", pe_bc)] if cc == 0 else [], inc=0)
                t.emit("dv", lambda e, ygc=ygc, gcol=gcol: e.scalar_tensor_tensor(
                    out=ygc, in0=PS[3], scalar=gcol, in1=GS2, op0=ALU.mult, op1=ALU.add),
                    inc=0)
                t.emit("dv", lambda e, ygc=ygc, cc=cc: e.tensor_scalar(
                    out=ygc, in0=ygc, scalar1=CBET[:, cc:cc + 1], scalar2=None, op0=ALU.add),
                    inc=(1 if cc == 15 else 0))
            dv_yn = t.snap("dv")
            ps_dep[2] = ("dv", dv_yn)
            ps_dep[3] = ("dv", dv_yn)
            return dv_yn

        def wout_block(dv_yn):
            for oc in range(16):
                slot = oc % 4
                bank = oc % 2
                gp_l = t.emit("gp", lambda e, oc=oc, slot=slot: e.dma_start(
                    out=A2[:, slot * 2048:(slot + 1) * 2048].rearrange("p (k c) -> p k c", k=NK),
                    in_=wost[oc]),
                    waits=[a2_dep[slot]] if a2_dep[slot] else [], incs=[(f"gp{slot}", 16)])
                for k in range(NK):
                    t.emit("pe", lambda e, k=k, slot=slot, bank=bank: e.matmul(
                        out=PS[bank],
                        lhsT=A2[:, slot * 2048 + k * 128:slot * 2048 + (k + 1) * 128],
                        rhs=YG[:, k * TB:(k + 1) * TB],
                        start=(k == 0), stop=(k == NK - 1)),
                        waits=([(f"gp{slot}", gp_l), ("dv", dv_yn)]
                               + ([ps_dep[bank]] if ps_dep[bank] else [])) if k == 0 else [],
                        inc=(1 if k == NK - 1 else 0))
                pe_mm = t.snap("pe")
                a2_dep[slot] = ("pe", pe_mm)
                t.emit("ac", lambda e, oc=oc, bank=bank: e.activation(
                    out=SS[:, oc * TB:(oc + 1) * TB], in_=PS[bank], func=AF.Copy),
                    waits=[("pe", pe_mm)] + ([ss_guard[0]] if oc == 0 and ss_guard[0] else []),
                    inc=1)
                ps_dep[bank] = ("ac", t.snap("ac"))
            yg_guard[0] = ("pe", t.snap("pe"))

        def ffn_block(xn_dep):
            a1_free = [("pe", t.snap("pe")), ("dv", t.snap("dv")), ("ac", t.snap("ac")),
                       ("dv", xn_dep)]
            dv_fuse = 0
            for ic in range(64):
                sg_ = ic % 2
                su_ = 2 + ic % 2
                gbank = ic % 2
                ubank = 2 + ic % 2
                gp_g = t.emit("gp", lambda e, ic=ic, sg_=sg_: e.dma_start(
                    out=A2[:, sg_ * 2048:(sg_ + 1) * 2048].rearrange("p (k c) -> p k c", k=NK),
                    in_=wgst[ic]),
                    waits=[a2_dep[sg_]] if a2_dep[sg_] else [], incs=[(f"gp{sg_}", 16)])
                gp_u = t.emit("gp", lambda e, ic=ic, su_=su_: e.dma_start(
                    out=A2[:, su_ * 2048:(su_ + 1) * 2048].rearrange("p (k c) -> p k c", k=NK),
                    in_=wust[ic]),
                    waits=[a2_dep[su_]] if a2_dep[su_] else [], incs=[(f"gp{su_}", 16)])
                for k in range(NK):
                    t.emit("pe", lambda e, k=k, sg_=sg_, gbank=gbank: e.matmul(
                        out=PS[gbank],
                        lhsT=A2[:, sg_ * 2048 + k * 128:sg_ * 2048 + (k + 1) * 128],
                        rhs=XN[:, k * TB:(k + 1) * TB],
                        start=(k == 0), stop=(k == NK - 1)),
                        waits=([(f"gp{sg_}", gp_g)] + ([ps_dep[gbank]] if ps_dep[gbank] else [])
                               + (a1_free if ic == 0 else [])) if k == 0 else [],
                        inc=(1 if k == NK - 1 else 0))
                pe_g = t.snap("pe")
                for k in range(NK):
                    t.emit("pe", lambda e, k=k, su_=su_, ubank=ubank: e.matmul(
                        out=PS[ubank],
                        lhsT=A2[:, su_ * 2048 + k * 128:su_ * 2048 + (k + 1) * 128],
                        rhs=XN[:, k * TB:(k + 1) * TB],
                        start=(k == 0), stop=(k == NK - 1)),
                        waits=([(f"gp{su_}", gp_u)]
                               + ([ps_dep[ubank]] if ps_dep[ubank] else [])) if k == 0 else [],
                        inc=(1 if k == NK - 1 else 0))
                pe_u = t.snap("pe")
                a2_dep[sg_] = ("pe", pe_g)
                a2_dep[su_] = ("pe", pe_u)
                t.emit("ac", lambda e, ic=ic, gbank=gbank: e.activation(
                    out=GS[:, (ic % 2) * 512:(ic % 2) * 512 + 512], in_=PS[gbank],
                    func=AF.Silu, scale=CSG[:, ic:ic + 1]),
                    waits=[("pe", pe_g)]
                          + ([dep.get(("gs", ic % 2))] if dep.get(("gs", ic % 2)) else []),
                    inc=1)
                ac_g = t.snap("ac")
                t.emit("dv", lambda e, ic=ic, ubank=ubank: e.scalar_tensor_tensor(
                    out=A1[:, ic * TB:(ic + 1) * TB], in0=PS[ubank],
                    scalar=CSU[:, ic:ic + 1],
                    in1=GS[:, (ic % 2) * 512:(ic % 2) * 512 + 512],
                    op0=ALU.mult, op1=ALU.mult),
                    waits=[("pe", pe_u), ("ac", ac_g)], inc=1)
                dv_fuse = t.snap("dv")
                dep[("gs", ic % 2)] = ("dv", dv_fuse)
                ps_dep[gbank] = ("ac", ac_g)
                ps_dep[ubank] = ("dv", dv_fuse)
            return dv_fuse

        def f2_block(blk, dv_fuse):
            for oc in range(16):
                bank = oc % 2
                gp_q = []
                for hhalf in range(2):
                    slot = 2 * hhalf
                    wl = ([a2_dep[slot]] if a2_dep[slot] else []) +                          ([a2_dep[slot + 1]] if a2_dep[slot + 1] else [])
                    gp_l = t.emit("gp", lambda e, oc=oc, hhalf=hhalf, slot=slot: e.dma_start(
                        out=A2[:, slot * 2048:(slot + 2) * 2048].rearrange(
                            "p (q k c) -> p q k c", q=2, k=NK),
                        in_=wdst[oc, 2 * hhalf:2 * hhalf + 2].rearrange(
                            "q p k c -> p q k c")),
                        waits=wl, incs=[(f"gp{slot}", 16)])
                    gp_q += [gp_l, gp_l]
                xr_dep = xr_guard[oc % 2]
                sy_x = t.emit("sy", lambda e, oc=oc, blk=blk: e.dma_start(
                    out=XR[:, (oc % 2) * 512:(oc % 2) * 512 + 512], in_=xst[blk, oc]),
                    waits=[xr_dep] if xr_dep else [], incs=[(f"syx{oc % 2}", 16)])
                for k in range(64):
                    q = k // NK
                    kk = k % NK
                    t.emit("pe", lambda e, k=k, q=q, kk=kk, bank=bank: e.matmul(
                        out=PS[bank],
                        lhsT=A2[:, q * 2048 + kk * 128:q * 2048 + (kk + 1) * 128],
                        rhs=A1[:, k * TB:(k + 1) * TB],
                        start=(k == 0), stop=(k == 63)),
                        waits=([(f"gp{2 * (q // 2)}", gp_q[q]), ("dv", dv_fuse)]
                               + ([ps_dep[bank]] if ps_dep[bank] else []))
                              if kk == 0 else [],
                        inc=(1 if k == 63 else 0))
                pe_mm = t.snap("pe")
                for q in range(4):
                    a2_dep[q] = ("pe", pe_mm)
                t.emit("ac", lambda e, oc=oc, bank=bank: e.activation(
                    out=GS[:, (oc % 2) * 512:(oc % 2) * 512 + 512], in_=PS[bank],
                    func=AF.Silu, scale=CSD[:, oc:oc + 1]),
                    waits=[("pe", pe_mm)]
                          + ([dep.get(("gs", oc % 2))] if dep.get(("gs", oc % 2)) else []),
                    inc=1)
                ac_g = t.snap("ac")
                ps_dep[bank] = ("ac", ac_g)
                sl = oc % 4
                a3t = A3[:, sl * 512:(sl + 1) * 512]
                t.emit("dv", lambda e, oc=oc: e.tensor_mul(
                    out=GS2, in0=SS[:, oc * TB:(oc + 1) * TB],
                    in1=GS[:, (oc % 2) * 512:(oc % 2) * 512 + 512]),
                    waits=[("ac", ac_g), (f"syx{oc % 2}", sy_x)], inc=0)
                t.emit("dv", lambda e, a3t=a3t, oc=oc: e.tensor_add(
                    out=a3t, in0=GS2, in1=XR[:, (oc % 2) * 512:(oc % 2) * 512 + 512]),
                    waits=a3_guard[sl], inc=1)
                dv_o = t.snap("dv")
                dep[("gs", oc % 2)] = ("dv", dv_o)
                xr_guard[oc % 2] = ("dv", dv_o)
                if oc % 4 == 3:
                    g = oc // 4
                    sy_st = t.emit("sy", lambda e, blk=blk, g=g: e.dma_start(
                        out=outst[blk, 4 * g:4 * g + 4].rearrange("a p t -> p a t"),
                        in_=A3.rearrange("p (a t) -> p a t", a=4)),
                        waits=[("dv", dv_o)], incs=[("syst", 16)])
                    for sl2 in range(4):
                        a3_guard[sl2] = [("syst", sy_st)]
            ss_guard[0] = ("dv", t.snap("dv"))

        pend = None
        for pblk in range(NBLK):
            dv_norm = ln_stats(pend) if pend is not None else ln_block(xpst, pblk)
            cols = list(range(0, 16)) + list(range(32, 48)) + [64]
            for ci, cc in enumerate(cols):
                kind = "x" if cc < 16 else ("b" if cc < 48 else "a")
                proj_strip(cc, ci % 4, kind, dv_norm)
            xn_guard[0] = ("pe", t.snap("pe"))
            pend = ln_load(xpst, pblk + 1) if pblk + 1 < NBLK else ln_load(xst, 0)
            proj_dv = t.snap("dv")
            for tau in range(4):
                ssd_chunk(tau, summary_only=True, proj_dv=proj_dv)

        if DEBUG:
            t.emit("sy", lambda e: e.dma_start(out=dstf, in_=STF),
                   waits=[("dv", t.snap("dv"))], incs=[("syst", 16)])

        pending_ln = [pend]

        for blk in range(NBLK):
            if pending_ln[0] is None:
                dv_norm = ln_block(xst, blk)
            else:
                dv_norm = ln_stats(pending_ln[0])
                pending_ln[0] = None
            cols = (list(range(0, 16)) + list(range(32, 64)) + [64]
                    + list(range(16, 32)))
            for ci, cc in enumerate(cols):
                kind = ("x" if cc < 16 else "z" if cc < 32 else "b" if cc < 48
                        else "c" if cc < 64 else "a")
                proj_strip(cc, ci % 4, kind, dv_norm)
                if cc == 64:
                    la_ac_early = t.snap("ac")
                    proj_dv_early = t.snap("dv")
            proj_dv = proj_dv_early
            if DEBUG and blk == 0:
                t.emit("sy", lambda e: e.dma_start(out=dxn, in_=XN),
                       waits=[("dv", dv_norm)], incs=[("syst", 16)])
                t.emit("sy", lambda e: e.dma_start(out=dlap, in_=LAP),
                       waits=[("ac", t.snap("ac")), ("dv", proj_dv)], incs=[("syst", 16)])
                t.emit("sy", lambda e: e.dma_start(out=da1, in_=A1),
                       waits=[("ac", t.snap("ac")), ("dv", proj_dv)], incs=[("syst", 16)])
            for tau in range(4):
                ssd_chunk(tau, summary_only=False, proj_dv=proj_dv)
            if DEBUG and blk == 0:
                t.emit("sy", lambda e: e.dma_start(out=dncf, in_=NCF),
                       waits=[("dv", t.snap("dv"))], incs=[("syst", 16)])
                t.emit("sy", lambda e: e.dma_start(out=dnct, in_=NCT),
                       waits=[("ac", t.snap("ac"))], incs=[("syst", 16)])
                t.emit("sy", lambda e: e.dma_start(out=dyg, in_=YG),
                       waits=[("dv", t.snap("dv"))], incs=[("syst", 16)])
            dv_yn = yln_block()
            wout_block(dv_yn)
            if DEBUG and blk == 0:
                t.emit("sy", lambda e: e.dma_start(out=dyn, in_=YG),
                       waits=[("dv", dv_yn)], incs=[("syst", 16)])
                t.emit("sy", lambda e: e.dma_start(out=dss, in_=SS),
                       waits=[("ac", t.snap("ac"))], incs=[("syst", 16)])
            dv_fuse = ffn_block(dv_norm)
            if DEBUG and blk == 0:
                t.emit("sy", lambda e: e.dma_start(out=dfuse, in_=A1[:, 0:4096]),
                       waits=[("dv", dv_fuse)], incs=[("syst", 16)])
            xn_guard[0] = ("pe", t.snap("pe"))
            if blk + 1 < NBLK:
                pending_ln[0] = ln_load(xst, blk + 1)
            f2_block(blk, dv_fuse)

        t.emit("sy", lambda e: e.wait_ge(SEM["syst"], t.cnt["syst"]))

        def replay(ek):
            def run(eng):
                hw = {}
                for fn, waits, incs in t.tape[ek]:
                    for (sk, v) in waits:
                        if v > hw.get(sk, 0):
                            eng.wait_ge(SEM[sk], v)
                            hw[sk] = v
                    ins = fn(eng)
                    if incs and ins is not None:
                        for sk, amt in incs:
                            ins = ins.then_inc(SEM[sk], amt)
            return run

        block.sync(replay("sy"))
        block.gpsimd(replay("gp"))
        block.tensor(replay("pe"))
        block.vector(replay("dv"))
        block.scalar(replay("ac"))

    return nc


_NC_CACHE = {}


def _stage_shared(W_in, W_out, gamma, beta, D, w_gate, w_up, w_down, s_gate, s_up, s_down):
    W_in = np.asarray(W_in, np.float32)
    W_in_pad = np.zeros((65 * 128, DM), np.float32)
    W_in_pad[:8224] = W_in
    wst = np.ascontiguousarray(W_in_pad.reshape(65, 128, NK, 128).transpose(0, 3, 2, 1))
    W_out = np.asarray(W_out, np.float32)
    wost = np.ascontiguousarray(W_out.reshape(16, 128, NK, 128).transpose(0, 3, 2, 1))
    wg = np.asarray(w_gate)
    wgst = np.ascontiguousarray(wg.reshape(NK, 128, 64, 128).transpose(2, 1, 0, 3))
    wu = np.asarray(w_up)
    wust = np.ascontiguousarray(wu.reshape(NK, 128, 64, 128).transpose(2, 1, 0, 3))
    wd = np.asarray(w_down)
    wdst = np.ascontiguousarray(
        wd.reshape(4, NK, 128, 16, 128).transpose(3, 0, 2, 1, 4))
    gamv = np.ascontiguousarray(np.asarray(gamma, np.float32).reshape(16, 128).T)
    betv = np.ascontiguousarray(np.asarray(beta, np.float32).reshape(16, 128).T)
    sgv = np.ascontiguousarray(np.asarray(s_gate, np.float32).reshape(64, 128).T)
    suv = np.ascontiguousarray(np.asarray(s_up, np.float32).reshape(64, 128).T)
    sdv = np.ascontiguousarray(np.asarray(s_down, np.float32).reshape(16, 128).T)
    d2 = np.ascontiguousarray(np.tile(np.asarray(D, np.float32).T, (2, 1)))
    nege32 = np.zeros((32, 32 * 128), np.float32)
    for h in range(32):
        nege32[h, h * 128:(h + 1) * 128] = -1.0
    identb = np.eye(128).astype(ml_dtypes.bfloat16)
    identf = np.eye(32, dtype=np.float32)
    maskt = np.tril(np.full((128, 128), np.float32(MASKVAL), dtype=np.float32), k=-1
                    ).astype(ml_dtypes.bfloat16)
    onesb = np.ones((128, 1), ml_dtypes.bfloat16)
    onesr = np.ones((1, 128), np.float32)
    return dict(wst=wst, wost=wost, wgst=wgst, wust=wust, wdst=wdst,
                gam=gamv, bet=betv, sgm=sgv, sum_=suv, sdm=sdv, d2m=d2,
                nege32=nege32, identb=identb, identf=identf,
                maskt=maskt, onesb=onesb, onesr=onesr,
                fconst=np.tile(np.array([[EPS, 1.0 / DM, -0.5, -1.0, 0.0, 1.0, 0.0,
                                          0.0]], np.float32), (128, 1)))


def _stage_x(xslice):
    xT = np.ascontiguousarray(np.asarray(xslice, np.float32).T)
    return np.ascontiguousarray(xT.reshape(NK, 128, NBLK, TB).transpose(2, 0, 1, 3))


def kernel(L0_attn_x, L0_attn_W_in, L0_attn_W_out, L0_attn_gamma, L0_attn_beta,
           L0_attn_D, L0_ffn_w_gate, L0_ffn_w_up, L0_ffn_w_down,
           L0_ffn_s_gate, L0_ffn_s_up, L0_ffn_s_down):
    x = np.asarray(L0_attn_x, np.float32)
    shared = _stage_shared(L0_attn_W_in, L0_attn_W_out, L0_attn_gamma, L0_attn_beta,
                           L0_attn_D, L0_ffn_w_gate, L0_ffn_w_up, L0_ffn_w_down,
                           L0_ffn_s_gate, L0_ffn_s_up, L0_ffn_s_down)
    zeros_prev = np.zeros((NBLK, NK, 128, TB), np.float32)
    in_maps = []
    for c in range(NCORES):
        b, half = c // 2, c % 2
        xs = _stage_x(x[b, half * T:(half + 1) * T, :])
        xp = _stage_x(x[b, 0:T, :]) if half == 1 else zeros_prev
        in_maps.append(dict(xst=xs, xpst=xp, **shared))

    if "nc" not in _NC_CACHE:
        _NC_CACHE["nc"] = build_program()
    nc = _NC_CACHE["nc"]
    res = run_bass_kernel_spmd(nc, in_maps, core_ids=list(range(NCORES)))

    out = np.empty((4, 4096, DM), np.float32)
    for c in range(NCORES):
        b, half = c // 2, c % 2
        o = res.results[c]["outst"]
        ot = np.ascontiguousarray(o.transpose(0, 3, 1, 2)).reshape(T, DM)
        out[b, half * T:(half + 1) * T, :] = ot
    return out




# version 21
# speedup vs baseline: 1.1928x; 1.1468x over previous
import contextlib
import numpy as np
import ml_dtypes
import concourse.bass as bass
import concourse.mybir as mybir
from concourse.bass_utils import run_bass_kernel_spmd

F32 = mybir.dt.float32
BF16 = mybir.dt.bfloat16
I8 = mybir.dt.int8
AF = mybir.ActivationFunctionType
ALU = mybir.AluOpType

NCORES = 8
T = 2048
DM = 2048
NBLK = 4
TB = 512
NK = 16
CH = 128
EPS = 1e-5
MASKVAL = -1e9

ENGS = ("sy", "gp", "pe", "dv", "ac")
SEMS = ("pe", "dv", "ac", "gpc", "gpx",
        "syc", "sya0", "sya1", "sya2", "sya3", "syx0", "syx1", "syst",
        "gp0", "gp1", "gp2", "gp3")
DEBUG = False


class Tape:
    def __init__(self):
        self.tape = {k: [] for k in ENGS}
        self.cnt = {k: 0 for k in SEMS}

    def emit(self, ek, fn, waits=(), inc=0, incs=None):
        w = [(sk, v) for (sk, v) in waits if v and v > 0]
        if incs is None:
            incs = [(ek, inc)] if inc else []
        self.tape[ek].append((fn, w, incs))
        ret = 0
        for sk, amt in incs:
            self.cnt[sk] += amt
            ret = self.cnt[sk]
        return ret

    def snap(self, ek):
        return self.cnt[ek]


def build_program():
    nc = bass.Bass("TRN2", target_bir_lowering=False, debug=False)

    xst = nc.dram_tensor("xst", [NBLK, NK, 128, TB], F32, kind="ExternalInput").ap()
    xpst = nc.dram_tensor("xpst", [NBLK, NK, 128, TB], F32, kind="ExternalInput").ap()
    wst = nc.dram_tensor("wst", [65, 128, NK, 128], F32, kind="ExternalInput").ap()
    wost = nc.dram_tensor("wost", [16, 128, NK, 128], F32, kind="ExternalInput").ap()
    wgst = nc.dram_tensor("wgst", [64, 128, NK, 128], I8, kind="ExternalInput").ap()
    wust = nc.dram_tensor("wust", [64, 128, NK, 128], I8, kind="ExternalInput").ap()
    wdst = nc.dram_tensor("wdst", [16, 4, 128, NK, 128], I8, kind="ExternalInput").ap()
    gam = nc.dram_tensor("gam", [128, 16], F32, kind="ExternalInput").ap()
    bet = nc.dram_tensor("bet", [128, 16], F32, kind="ExternalInput").ap()
    sgm = nc.dram_tensor("sgm", [128, 64], F32, kind="ExternalInput").ap()
    sum_ = nc.dram_tensor("sum_", [128, 64], F32, kind="ExternalInput").ap()
    sdm = nc.dram_tensor("sdm", [128, 16], F32, kind="ExternalInput").ap()
    d2m = nc.dram_tensor("d2m", [128, 32], F32, kind="ExternalInput").ap()
    nege32 = nc.dram_tensor("nege32", [32, 32 * 128], F32, kind="ExternalInput").ap()
    identb = nc.dram_tensor("identb", [128, 128], BF16, kind="ExternalInput").ap()
    identf = nc.dram_tensor("identf", [32, 32], F32, kind="ExternalInput").ap()
    maskt = nc.dram_tensor("maskt", [128, 128], BF16, kind="ExternalInput").ap()
    onesb = nc.dram_tensor("onesb", [128, 1], BF16, kind="ExternalInput").ap()
    onesr = nc.dram_tensor("onesr", [1, 128], F32, kind="ExternalInput").ap()
    fconst = nc.dram_tensor("fconst", [128, 8], F32, kind="ExternalInput").ap()
    outst = nc.dram_tensor("outst", [NBLK, 16, 128, TB], F32, kind="ExternalOutput").ap()
    if DEBUG:
        dxn = nc.dram_tensor("dxn", [128, 8192], BF16, kind="ExternalOutput").ap()
        dlap = nc.dram_tensor("dlap", [32, 512], F32, kind="ExternalOutput").ap()
        dncf = nc.dram_tensor("dncf", [32, 512], F32, kind="ExternalOutput").ap()
        dnct = nc.dram_tensor("dnct", [128, 128], F32, kind="ExternalOutput").ap()
        dyg = nc.dram_tensor("dyg", [128, 8192], BF16, kind="ExternalOutput").ap()
        dyn = nc.dram_tensor("dyn", [128, 8192], BF16, kind="ExternalOutput").ap()
        dss = nc.dram_tensor("dss", [128, 8192], BF16, kind="ExternalOutput").ap()
        dstf = nc.dram_tensor("dstf", [64, 2048], F32, kind="ExternalOutput").ap()
        dfuse = nc.dram_tensor("dfuse", [128, 4096], BF16, kind="ExternalOutput").ap()
        da1 = nc.dram_tensor("da1", [128, 32768], BF16, kind="ExternalOutput").ap()

    with contextlib.ExitStack() as stack:
        sb = lambda n, s, d: stack.enter_context(nc.sbuf_tensor(n, s, d)).ap()

        A1 = sb("A1", [128, 32768], BF16)
        A2 = sb("A2", [128, 8192], BF16)
        A3 = sb("A3", [128, 2048], F32)
        XN = sb("XN", [128, 8192], BF16)
        YG = sb("YG", [128, 8192], BF16)
        SS = sb("SS", [128, 8192], BF16)
        GS = sb("GS", [128, 1024], F32)
        GS2 = sb("GS2", [128, 512], F32)
        XR = sb("XR", [128, 1024], F32)
        X2 = sb("X2", [128, 1024], BF16)
        MT = sb("MT", [128, 512], F32)
        STt = sb("STt", [128, 512], BF16)
        DE = sb("DE", [128, 512], F32)
        CDt = sb("CDt", [64, 512], BF16)
        BW = sb("BW", [128, 256], BF16)
        XT = sb("XT", [128, 512], BF16)
        TMP = sb("TMP", [128, 512], F32)
        NCF = sb("NCF", [32, 512], F32)
        LAP = sb("LAP", [32, 512], F32)
        NCT = sb("NCT", [128, 128], F32)
        WCOL = sb("WCOL", [128, 4], F32)
        LCOL = sb("LCOL", [64, 4], F32)
        ONES32 = sb("ONES32", [32, 128], F32)
        ROWS = sb("ROWS", [1, 3584], F32)
        STF = sb("STF", [64, 2048], F32)
        STB = sb("STB", [64, 2048], BF16)
        CN32 = sb("CN32", [32, 32 * 128], F32)
        CIDB = sb("CIDB", [128, 128], BF16)
        CIDF = sb("CIDF", [32, 32], F32)
        CMSK = sb("CMSK", [128, 128], BF16)
        CONB = sb("CONB", [128, 1], BF16)
        CONR = sb("CONR", [1, 128], F32)
        CGAM = sb("CGAM", [128, 16], F32)
        CBET = sb("CBET", [128, 16], F32)
        CSG = sb("CSG", [128, 64], F32)
        CSU = sb("CSU", [128, 64], F32)
        CSD = sb("CSD", [128, 16], F32)
        CD2 = sb("CD2", [128, 32], F32)
        CFC = sb("CFC", [128, 8], F32)
        for i, v in enumerate([EPS, 1.0 / DM, -0.5, -1.0, 0.0, 1.0]):
            nc.const_aps.register(F32, v, CFC[:, i:i + 1])

        PS = [stack.enter_context(nc.psum_tensor(f"ps{i}", [128, 512], F32)).ap()
              for i in range(8)]
        PSB = {i: PS[i].bitcast(BF16) for i in range(8)}

        SEM = {k: stack.enter_context(nc.semaphore(f"sem_{k}")) for k in SEMS}
        block = stack.enter_context(nc.Block())

        t = Tape()
        dep = {}
        ps_dep = {i: None for i in range(8)}
        a2_dep = {i: None for i in range(4)}
        xn_guard = [None]
        a3_guard = {i: [] for i in range(4)}
        xr_guard = {0: None, 1: None}
        ss_guard = [None]
        yg_guard = [None]

        for (dst, src) in [(CN32, nege32), (CIDB, identb), (CIDF, identf),
                           (CMSK, maskt), (CONB, onesb), (CONR, onesr),
                           (CGAM, gam), (CBET, bet), (CSG, sgm), (CSU, sum_),
                           (CSD, sdm), (CD2, d2m), (CFC, fconst)]:
            t.emit("sy", lambda e, d=dst, s=src: e.dma_start(out=d, in_=s),
                   incs=[("syc", 16)])
        consts_sy = t.snap("syc")
        for ek in ("pe", "dv", "ac", "gp"):
            t.emit(ek, lambda e: None, waits=[("syc", consts_sy)])
        t.emit("dv", lambda e: e.memset(ONES32, 1.0), inc=1)
        t.emit("dv", lambda e: e.memset(STF, 0.0), inc=1)
        t.emit("dv", lambda e: e.memset(STB, 0.0), inc=1)

        def rows_pipeline(wait_pe):
            mu = ROWS[:, 0:512]; e2 = ROWS[:, 512:1024]; mu2 = ROWS[:, 1024:1536]
            var = ROWS[:, 1536:2048]; lnv = ROWS[:, 2048:2560]
            r1 = ROWS[:, 2560:3072]; r0n = ROWS[:, 3072:3584]
            t.emit("ac", lambda e: e.activation(out=mu, in_=PS[2][0:1, :], func=AF.Copy,
                                                scale=1.0 / DM), waits=[("pe", wait_pe)], inc=1)
            t.emit("ac", lambda e: e.activation(out=e2, in_=PS[3][0:1, :], func=AF.Copy,
                                                scale=1.0 / DM), inc=1)
            t.emit("ac", lambda e: e.activation(out=mu2, in_=mu, func=AF.Square), inc=1)
            ac_r = t.snap("ac")
            t.emit("dv", lambda e: e.tensor_sub(out=var, in0=e2, in1=mu2),
                   waits=[("ac", ac_r)], inc=1)
            dv_r = t.snap("dv")
            t.emit("ac", lambda e: e.activation(out=lnv, in_=var, func=AF.Ln, bias=EPS),
                   waits=[("dv", dv_r)], inc=1)
            t.emit("ac", lambda e: e.activation(out=r1, in_=lnv, func=AF.Exp, scale=-0.5),
                   inc=1)
            ac_r1 = t.snap("ac")
            t.emit("dv", lambda e: e.scalar_tensor_tensor(
                out=r0n, in0=mu, scalar=-1.0, in1=r1, op0=ALU.mult, op1=ALU.mult),
                waits=[("ac", ac_r1)], inc=1)
            dv_r0 = t.snap("dv")
            return ac_r1, dv_r0, r1, r0n

        def ln_load(stage, blk):
            return t.emit("gp", lambda e, blk=blk: e.dma_start(
                out=XN.rearrange("p (k t) -> p k t", k=NK),
                in_=stage[blk].rearrange("k p t -> p k t")),
                waits=([xn_guard[0]] if xn_guard[0] else []), incs=[("gpx", 16)])

        def ln_stats(gp_x):
            for k in range(NK):
                xk = XN[:, k * TB:(k + 1) * TB]
                x2s = X2[:, (k % 2) * TB:(k % 2 + 1) * TB]
                t.emit("ac", lambda e, xk=xk, x2s=x2s: e.activation(
                    out=x2s, in_=xk, func=AF.Square),
                    waits=[("gpx", gp_x)]
                          + ([dep.get(("x2", k % 2))] if dep.get(("x2", k % 2)) else []),
                    inc=1)
                ack = t.snap("ac")
                t.emit("pe", lambda e, xk=xk, k=k: e.matmul(
                    out=PS[2][0:1, :], lhsT=CONB, rhs=xk,
                    start=(k == 0), stop=(k == NK - 1)),
                    waits=[("gpx", gp_x)] + ([ps_dep[2]] if k == 0 and ps_dep[2] else []),
                    inc=0)
                t.emit("pe", lambda e, x2s=x2s, k=k: e.matmul(
                    out=PS[3][0:1, :], lhsT=CONB, rhs=x2s,
                    start=(k == 0), stop=(k == NK - 1)),
                    waits=[("ac", ack)] + ([ps_dep[3]] if k == 0 and ps_dep[3] else []),
                    inc=1)
                dep[("x2", k % 2)] = ("pe", t.snap("pe"))
            pe_stats = t.snap("pe")
            ac_r1, dv_r0, r1, r0n = rows_pipeline(pe_stats)
            t.emit("pe", lambda e, r1=r1: e.matmul(out=PS[2], lhsT=CONR, rhs=r1,
                                                   start=True, stop=True),
                   waits=[("ac", ac_r1)], inc=0)
            t.emit("pe", lambda e, r0n=r0n: e.matmul(out=PS[3], lhsT=CONR, rhs=r0n,
                                                     start=True, stop=True),
                   waits=[("dv", dv_r0)], inc=1)
            pe_bc = t.snap("pe")
            for k in range(NK):
                xk = XN[:, k * TB:(k + 1) * TB]
                t.emit("dv", lambda e, xk=xk: e.tensor_mul(out=xk, in0=xk, in1=PS[2]),
                       waits=[("pe", pe_bc)] if k == 0 else [], inc=0)
                t.emit("dv", lambda e, xk=xk: e.tensor_add(out=xk, in0=xk, in1=PS[3]),
                       inc=(1 if k == NK - 1 else 0))
            dv_norm = t.snap("dv")
            ps_dep[2] = ("dv", dv_norm)
            ps_dep[3] = ("dv", dv_norm)
            return dv_norm

        def ln_block(stage, blk):
            return ln_stats(ln_load(stage, blk))

        def proj_strip(cc, slot, kind, xn_dep):
            bank = slot
            gp_l = t.emit("gp", lambda e, cc=cc, slot=slot: e.dma_start(
                out=A2[:, slot * 2048:(slot + 1) * 2048].rearrange("p (k c) -> p k c", k=NK),
                in_=wst[cc]),
                waits=[a2_dep[slot]] if a2_dep[slot] else [], incs=[(f"gp{slot}", 16)])
            for k in range(NK):
                t.emit("pe", lambda e, k=k, slot=slot, bank=bank: e.matmul(
                    out=PS[bank],
                    lhsT=A2[:, slot * 2048 + k * 128:slot * 2048 + (k + 1) * 128],
                    rhs=XN[:, k * TB:(k + 1) * TB],
                    start=(k == 0), stop=(k == NK - 1)),
                    waits=([(f"gp{slot}", gp_l), ("dv", xn_dep)]
                           + ([ps_dep[bank]] if ps_dep[bank] else [])) if k == 0 else [],
                    inc=(1 if k == NK - 1 else 0))
            pe_mm = t.snap("pe")
            a2_dep[slot] = ("pe", pe_mm)
            if kind == "x":
                dst = A1[:, cc * TB:(cc + 1) * TB]
            elif kind == "z":
                dst = A1[:, 16384 + (cc - 16) * TB:16384 + (cc - 15) * TB]
            elif kind == "b":
                dst = A1[:, 8192 + (cc - 32) * TB:8192 + (cc - 31) * TB]
            elif kind == "c":
                dst = A1[:, 24576 + (cc - 48) * TB:24576 + (cc - 47) * TB]
            if kind in ("x", "b"):
                t.emit("dv", lambda e, dst=dst, bank=bank: e.tensor_copy(out=dst, in_=PS[bank]),
                       waits=[("pe", pe_mm)], inc=1)
                ps_dep[bank] = ("dv", t.snap("dv"))
            elif kind == "z":
                t.emit("ac", lambda e, dst=dst, bank=bank: e.activation(
                    out=dst, in_=PS[bank], func=AF.Silu), waits=[("pe", pe_mm)], inc=1)
                ps_dep[bank] = ("ac", t.snap("ac"))
            elif kind == "c":
                t.emit("ac", lambda e, dst=dst, bank=bank: e.activation(
                    out=dst, in_=PS[bank], func=AF.Copy), waits=[("pe", pe_mm)], inc=1)
                ps_dep[bank] = ("ac", t.snap("ac"))
            elif kind == "a":
                t.emit("ac", lambda e, bank=bank: e.activation(
                    out=LAP, in_=PS[bank][0:32, :], func=AF.Exp),
                    waits=[("pe", pe_mm)], inc=1)
                ps_dep[bank] = ("ac", t.snap("ac"))

        def xf_sl(h, tau):
            o = (h // 2) * TB + tau * CH
            return A1[(h % 2) * 64:(h % 2) * 64 + 64, o:o + CH]

        def b_sl(h, tau):
            o = 8192 + (h // 2) * TB + tau * CH
            return A1[(h % 2) * 64:(h % 2) * 64 + 64, o:o + CH]

        def z_sl(h, tau):
            o = 16384 + (h // 2) * TB + tau * CH
            return A1[(h % 2) * 64:(h % 2) * 64 + 64, o:o + CH]

        def c_sl(h, tau):
            o = 24576 + (h // 2) * TB + tau * CH
            return A1[(h % 2) * 64:(h % 2) * 64 + 64, o:o + CH]

        def b_pair(hp, tau):
            o = 8192 + hp * TB + tau * CH
            return A1[:, o:o + CH]

        def x_pair(hp, tau):
            o = hp * TB + tau * CH
            return A1[:, o:o + CH]

        P_H = [4, 5, 0, 1]
        T_H = [6, 7, 2, 3]

        def ssd_chunk(tau, summary_only, proj_dv, la_ac=None, z_ac=None):
            if la_ac is None:
                la_ac = t.snap("ac")
            ncf_sl = NCF[:, tau * CH:(tau + 1) * CH]
            t.emit("dv", lambda e, ncf_sl=ncf_sl, tau=tau: e.tensor_tensor_scan(
                out=ncf_sl, data0=ONES32, data1=LAP[:, tau * CH:(tau + 1) * CH],
                initial=0.0, op0=ALU.mult, op1=ALU.add),
                waits=[("ac", la_ac)] + ([dep.get(("ncf", tau))] if dep.get(("ncf", tau)) else []),
                inc=1)
            dv_sc = t.snap("dv")
            t.emit("pe", lambda e, ncf_sl=ncf_sl: e.transpose(
                out=PS[6][:, 448:480], in_=ncf_sl, identity=CIDF),
                waits=[("dv", dv_sc)]
                      + ([dep.get(("trb", 6))] if dep.get(("trb", 6)) else [])
                      + ([dep.get("nct_evac")] if dep.get("nct_evac") else []),
                inc=1)
            pe_tr = t.snap("pe")
            nct_sl = NCT[:, tau * 32:(tau + 1) * 32]
            t.emit("ac", lambda e, nct_sl=nct_sl: e.copy(out=nct_sl, in_=PS[6][:, 448:480]),
                   waits=[("pe", pe_tr)], inc=1)
            dep["nct_evac"] = ("ac", t.snap("ac"))

            ctx = {}

            def front(i):
                h = i
                hp, j = i // 2, i % 2
                jp = h % 4
                pb_ = P_H[jp]
                tb_ = T_H[hp % 4]
                pbank = PSB[tb_]
                c = {"h": h, "hp": hp, "j": j, "jp": jp, "pb_": pb_, "tb_": tb_,
                     "pbank": pbank}
                if j == 0:
                    trw = ([dep.get(("trb", tb_))] if dep.get(("trb", tb_)) else [])
                    if tb_ == 6:
                        trw = trw + [dep["nct_evac"]]
                    if z_ac:
                        trw = trw + [("ac", z_ac)]
                    t.emit("pe", lambda e, hp=hp, tau=tau, pbank=pbank: e.transpose(
                        out=pbank[:, 0:128], in_=b_pair(hp, tau), identity=CIDB),
                        waits=trw + [("dv", proj_dv)], inc=0)
                    t.emit("pe", lambda e, hp=hp, tau=tau, pbank=pbank: e.transpose(
                        out=pbank[:, 128:256], in_=x_pair(hp, tau), identity=CIDB), inc=1)
                    pe_trp = t.snap("pe")
                    t.emit("ac", lambda e, hp=hp, pbank=pbank: e.copy(
                        out=XT[:, (hp % 4) * 128:(hp % 4) * 128 + 128],
                        in_=pbank[:, 128:256]),
                        waits=[("pe", pe_trp)]
                              + ([dep.get(("xt_use", hp % 4))] if dep.get(("xt_use", hp % 4))
                                 else []),
                        inc=1)
                    dep[("acxt", hp)] = t.snap("ac")
                c["ac_xt"] = dep[("acxt", hp)]
                if not summary_only:
                    t.emit("pe", lambda e, h=h, tau=tau, pb_=pb_: e.matmul(
                        out=PS[pb_][:, 192:320], lhsT=b_sl(h, tau),
                        rhs=c_sl(h, tau), start=True, stop=True),
                        waits=([ps_dep[pb_]] if ps_dep[pb_] else [])
                              + [("dv", proj_dv)], inc=0)
                t.emit("pe", lambda e, h=h, tau=tau, pb_=pb_: e.matmul(
                    out=PS[pb_][:, 320:448],
                    lhsT=CN32[:, h * 128:(h + 1) * 128],
                    rhs=NCF[:, tau * CH:(tau + 1) * CH],
                    start=True, stop=summary_only),
                    waits=[ps_dep[pb_]] if ps_dep[pb_] else [],
                    inc=1)
                pe_b1 = t.snap("pe")
                t.emit("ac", lambda e, jp=jp, pb_=pb_, tau=tau, h=h: e.activation(
                    out=WCOL[:, jp:jp + 1], in_=PS[pb_][:, 447:448], func=AF.Exp,
                    bias=NCT[:, tau * 32 + h:tau * 32 + h + 1]),
                    waits=[("pe", pe_b1)], inc=0)
                t.emit("ac", lambda e, jp=jp, pb_=pb_: e.activation(
                    out=LCOL[:, jp:jp + 1], in_=PS[pb_][0:64, 447:448], func=AF.Exp),
                    inc=(1 if summary_only else 0))
                if not summary_only:
                    t.emit("ac", lambda e, jp=jp, j=j, pb_=pb_: e.activation(
                        out=DE[j * 64:j * 64 + 64, jp * 128:jp * 128 + 128],
                        in_=PS[pb_][0:64, 320:448], func=AF.Exp), inc=1)
                c["ac_a"] = t.snap("ac")
                ctx[i] = c

            def mid(i):
                c = ctx[i]
                h, hp, j, jp = c["h"], c["hp"], c["j"], c["jp"]
                pb_, tb_, pbank = c["pb_"], c["tb_"], c["pbank"]
                if not summary_only:
                    t.emit("pe", lambda e, pb_=pb_: e.matmul(
                        out=PS[pb_][:, 320:448], lhsT=CIDB, rhs=CMSK,
                        start=False, stop=True),
                        waits=[("ac", c["ac_a"])], inc=1)
                    pe_msk = t.snap("pe")
                    t.emit("ac", lambda e, jp=jp, pb_=pb_, tau=tau, h=h: e.activation(
                        out=MT[:, jp * 128:jp * 128 + 128], in_=PS[pb_][:, 320:448],
                        func=AF.Exp, bias=NCT[:, tau * 32 + h:tau * 32 + h + 1]),
                        waits=[("pe", pe_msk)], inc=1)
                    ac_b = t.snap("ac")
                t.emit("dv", lambda e, jp=jp, j=j, pbank=pbank: e.tensor_scalar(
                    out=BW[:, jp * 64:jp * 64 + 64],
                    in0=pbank[:, j * 64:j * 64 + 64],
                    scalar1=WCOL[:, jp:jp + 1], scalar2=None, op0=ALU.mult),
                    waits=[("ac", c["ac_a"])], inc=(1 if summary_only else 0))
                if not summary_only:
                    t.emit("dv", lambda e, h=h, tau=tau, jp=jp, j=j: e.tensor_mul(
                        out=CDt[:, jp * 128:jp * 128 + 128], in0=c_sl(h, tau),
                        in1=DE[j * 64:j * 64 + 64, jp * 128:jp * 128 + 128]),
                        waits=([dep.get(("cd", jp))] if dep.get(("cd", jp)) else []),
                        inc=0)
                    t.emit("dv", lambda e, jp=jp, pb_=pb_: e.tensor_mul(
                        out=STt[:, jp * 128:jp * 128 + 128],
                        in0=PS[pb_][:, 192:320],
                        in1=MT[:, jp * 128:jp * 128 + 128]),
                        waits=[("ac", ac_b)], inc=1)
                c["dv_s"] = t.snap("dv")

            def back(i):
                c = ctx.pop(i)
                h, hp, j, jp = c["h"], c["hp"], c["j"], c["jp"]
                pb_, tb_ = c["pb_"], c["tb_"]
                dv_s, ac_xt = c["dv_s"], c["ac_xt"]
                stb_dep = dep.get(("stb", h))
                if not summary_only:
                    t.emit("pe", lambda e, jp=jp, hp=hp, j=j, pb_=pb_: e.matmul(
                        out=PS[pb_][0:64, 0:128],
                        lhsT=XT[:, (hp % 4) * 128 + j * 64:(hp % 4) * 128 + j * 64 + 64],
                        rhs=STt[:, jp * 128:jp * 128 + 128],
                        start=True, stop=False),
                        waits=[("dv", dv_s), ("ac", ac_xt)]
                              + ([stb_dep] if stb_dep else []), inc=0)
                    t.emit("pe", lambda e, h=h, jp=jp, pb_=pb_: e.matmul(
                        out=PS[pb_][0:64, 0:128],
                        lhsT=STB[:, h * 64:(h + 1) * 64],
                        rhs=CDt[:, jp * 128:jp * 128 + 128],
                        start=False, stop=True), inc=0)
                t.emit("pe", lambda e, jp=jp, hp=hp, j=j, pb_=pb_: e.matmul(
                    out=PS[pb_][0:64, 128:192],
                    lhsT=BW[:, jp * 64:jp * 64 + 64],
                    rhs=XT[:, (hp % 4) * 128 + j * 64:(hp % 4) * 128 + j * 64 + 64],
                    start=True, stop=True),
                    waits=([("dv", dv_s), ("ac", ac_xt)]
                           + ([dep.get(("stf", h))] if dep.get(("stf", h)) else []))
                          if summary_only else [],
                    inc=1)
                pe_b2 = t.snap("pe")
                if j == 1:
                    dep[("xt_use", hp % 4)] = ("pe", pe_b2)
                    dep[("trb", tb_)] = ("dv", dv_s)
                if not summary_only:
                    t.emit("dv", lambda e, h=h, tau=tau, jp=jp, j=j, pb_=pb_:
                           e.scalar_tensor_tensor(
                               out=TMP[j * 64:j * 64 + 64, jp * 128:jp * 128 + 128],
                               in0=xf_sl(h, tau),
                               scalar=CD2[j * 64:j * 64 + 64, h:h + 1],
                               in1=PS[pb_][0:64, 0:128],
                               op0=ALU.mult, op1=ALU.add),
                           waits=[("pe", pe_b2)], inc=0)
                    t.emit("dv", lambda e, h=h, tau=tau, jp=jp, j=j: e.tensor_mul(
                        out=YG[j * 64:j * 64 + 64,
                               (h // 2) * TB + tau * CH:(h // 2) * TB + tau * CH + CH],
                        in0=TMP[j * 64:j * 64 + 64, jp * 128:jp * 128 + 128],
                        in1=z_sl(h, tau)),
                        waits=[("ac", z_ac)] if z_ac else [], inc=0)
                    dep[("cd", jp)] = ("pe", pe_b2)
                t.emit("dv", lambda e, h=h, jp=jp: e.tensor_scalar(
                    out=STF[:, h * 64:(h + 1) * 64], in0=STF[:, h * 64:(h + 1) * 64],
                    scalar1=LCOL[:, jp:jp + 1], scalar2=None, op0=ALU.mult),
                    waits=[("pe", pe_b2)], inc=0)
                t.emit("dv", lambda e, h=h, pb_=pb_: e.tensor_add(
                    out=STF[:, h * 64:(h + 1) * 64], in0=STF[:, h * 64:(h + 1) * 64],
                    in1=PS[pb_][0:64, 128:192]), inc=0)
                t.emit("dv", lambda e, h=h: e.tensor_copy(
                    out=STB[:, h * 64:(h + 1) * 64], in_=STF[:, h * 64:(h + 1) * 64]),
                    inc=1)
                dv_st = t.snap("dv")
                dep[("stb", h)] = ("dv", dv_st)
                dep[("stf", h)] = ("dv", dv_st)
                ps_dep[pb_] = ("dv", dv_st)

            for i in range(34):
                if 1 <= i <= 32:
                    mid(i - 1)
                if i < 32:
                    front(i)
                if i >= 2:
                    back(i - 2)
            dep[("ncf", tau)] = ("pe", t.snap("pe"))

        def yln_block():
            ydep = t.snap("dv")
            for cc in range(16):
                ygc = YG[:, cc * TB:(cc + 1) * TB]
                y2s = X2[:, (cc % 2) * TB:(cc % 2 + 1) * TB]
                t.emit("ac", lambda e, ygc=ygc, y2s=y2s: e.activation(
                    out=y2s, in_=ygc, func=AF.Square),
                    waits=[("dv", ydep)]
                          + ([dep.get(("x2", cc % 2))] if dep.get(("x2", cc % 2)) else []),
                    inc=1)
                acy = t.snap("ac")
                t.emit("pe", lambda e, ygc=ygc, cc=cc: e.matmul(
                    out=PS[2][0:1, :], lhsT=CONB, rhs=ygc,
                    start=(cc == 0), stop=(cc == 15)),
                    waits=[("dv", ydep)] + ([ps_dep[2]] if cc == 0 and ps_dep[2] else []),
                    inc=0)
                t.emit("pe", lambda e, y2s=y2s, cc=cc: e.matmul(
                    out=PS[3][0:1, :], lhsT=CONB, rhs=y2s,
                    start=(cc == 0), stop=(cc == 15)),
                    waits=[("ac", acy)] + ([ps_dep[3]] if cc == 0 and ps_dep[3] else []),
                    inc=1)
                dep[("x2", cc % 2)] = ("pe", t.snap("pe"))
            pe_stats = t.snap("pe")
            ac_r1, dv_r0, r1, r0n = rows_pipeline(pe_stats)
            t.emit("pe", lambda e, r1=r1: e.matmul(out=PS[2], lhsT=CONR, rhs=r1,
                                                   start=True, stop=True),
                   waits=[("ac", ac_r1)], inc=0)
            t.emit("pe", lambda e, r0n=r0n: e.matmul(out=PS[3], lhsT=CONR, rhs=r0n,
                                                     start=True, stop=True),
                   waits=[("dv", dv_r0)], inc=1)
            pe_bc = t.snap("pe")
            for cc in range(16):
                ygc = YG[:, cc * TB:(cc + 1) * TB]
                gcol = CGAM[:, cc:cc + 1]
                t.emit("dv", lambda e, ygc=ygc, gcol=gcol: e.scalar_tensor_tensor(
                    out=GS2, in0=ygc, scalar=gcol, in1=PS[2], op0=ALU.mult, op1=ALU.mult),
                    waits=[("pe", pe_bc)] if cc == 0 else [], inc=0)
                t.emit("dv", lambda e, ygc=ygc, gcol=gcol: e.scalar_tensor_tensor(
                    out=ygc, in0=PS[3], scalar=gcol, in1=GS2, op0=ALU.mult, op1=ALU.add),
                    inc=0)
                t.emit("dv", lambda e, ygc=ygc, cc=cc: e.tensor_scalar(
                    out=ygc, in0=ygc, scalar1=CBET[:, cc:cc + 1], scalar2=None, op0=ALU.add),
                    inc=(1 if cc == 15 else 0))
            dv_yn = t.snap("dv")
            ps_dep[2] = ("dv", dv_yn)
            ps_dep[3] = ("dv", dv_yn)
            return dv_yn

        def wout_block(dv_yn):
            for oc in range(16):
                slot = oc % 4
                bank = oc % 4
                gp_l = t.emit("gp", lambda e, oc=oc, slot=slot: e.dma_start(
                    out=A2[:, slot * 2048:(slot + 1) * 2048].rearrange("p (k c) -> p k c", k=NK),
                    in_=wost[oc]),
                    waits=[a2_dep[slot]] if a2_dep[slot] else [], incs=[(f"gp{slot}", 16)])
                for k in range(NK):
                    t.emit("pe", lambda e, k=k, slot=slot, bank=bank: e.matmul(
                        out=PS[bank],
                        lhsT=A2[:, slot * 2048 + k * 128:slot * 2048 + (k + 1) * 128],
                        rhs=YG[:, k * TB:(k + 1) * TB],
                        start=(k == 0), stop=(k == NK - 1)),
                        waits=([(f"gp{slot}", gp_l), ("dv", dv_yn)]
                               + ([ps_dep[bank]] if ps_dep[bank] else [])) if k == 0 else [],
                        inc=(1 if k == NK - 1 else 0))
                pe_mm = t.snap("pe")
                a2_dep[slot] = ("pe", pe_mm)
                t.emit("ac", lambda e, oc=oc, bank=bank: e.activation(
                    out=SS[:, oc * TB:(oc + 1) * TB], in_=PS[bank], func=AF.Copy),
                    waits=[("pe", pe_mm)] + ([ss_guard[0]] if oc == 0 and ss_guard[0] else []),
                    inc=1)
                ps_dep[bank] = ("ac", t.snap("ac"))
            yg_guard[0] = ("pe", t.snap("pe"))

        def ffn_block(xn_dep):
            a1_free = [("pe", t.snap("pe")), ("dv", t.snap("dv")), ("ac", t.snap("ac")),
                       ("dv", xn_dep)]
            dv_fuse = 0
            for ic in range(64):
                sg_ = ic % 2
                su_ = 2 + ic % 2
                gbank = ic % 2
                ubank = 2 + ic % 2
                gp_g = t.emit("gp", lambda e, ic=ic, sg_=sg_: e.dma_start(
                    out=A2[:, sg_ * 2048:(sg_ + 1) * 2048].rearrange("p (k c) -> p k c", k=NK),
                    in_=wgst[ic]),
                    waits=[a2_dep[sg_]] if a2_dep[sg_] else [], incs=[(f"gp{sg_}", 16)])
                gp_u = t.emit("gp", lambda e, ic=ic, su_=su_: e.dma_start(
                    out=A2[:, su_ * 2048:(su_ + 1) * 2048].rearrange("p (k c) -> p k c", k=NK),
                    in_=wust[ic]),
                    waits=[a2_dep[su_]] if a2_dep[su_] else [], incs=[(f"gp{su_}", 16)])
                for k in range(NK):
                    t.emit("pe", lambda e, k=k, sg_=sg_, gbank=gbank: e.matmul(
                        out=PS[gbank],
                        lhsT=A2[:, sg_ * 2048 + k * 128:sg_ * 2048 + (k + 1) * 128],
                        rhs=XN[:, k * TB:(k + 1) * TB],
                        start=(k == 0), stop=(k == NK - 1)),
                        waits=([(f"gp{sg_}", gp_g)] + ([ps_dep[gbank]] if ps_dep[gbank] else [])
                               + (a1_free if ic == 0 else [])) if k == 0 else [],
                        inc=(1 if k == NK - 1 else 0))
                pe_g = t.snap("pe")
                for k in range(NK):
                    t.emit("pe", lambda e, k=k, su_=su_, ubank=ubank: e.matmul(
                        out=PS[ubank],
                        lhsT=A2[:, su_ * 2048 + k * 128:su_ * 2048 + (k + 1) * 128],
                        rhs=XN[:, k * TB:(k + 1) * TB],
                        start=(k == 0), stop=(k == NK - 1)),
                        waits=([(f"gp{su_}", gp_u)]
                               + ([ps_dep[ubank]] if ps_dep[ubank] else [])) if k == 0 else [],
                        inc=(1 if k == NK - 1 else 0))
                pe_u = t.snap("pe")
                a2_dep[sg_] = ("pe", pe_g)
                a2_dep[su_] = ("pe", pe_u)
                t.emit("ac", lambda e, ic=ic, gbank=gbank: e.activation(
                    out=GS[:, (ic % 2) * 512:(ic % 2) * 512 + 512], in_=PS[gbank],
                    func=AF.Silu, scale=CSG[:, ic:ic + 1]),
                    waits=[("pe", pe_g)]
                          + ([dep.get(("gs", ic % 2))] if dep.get(("gs", ic % 2)) else []),
                    inc=1)
                ac_g = t.snap("ac")
                t.emit("dv", lambda e, ic=ic, ubank=ubank: e.scalar_tensor_tensor(
                    out=A1[:, ic * TB:(ic + 1) * TB], in0=PS[ubank],
                    scalar=CSU[:, ic:ic + 1],
                    in1=GS[:, (ic % 2) * 512:(ic % 2) * 512 + 512],
                    op0=ALU.mult, op1=ALU.mult),
                    waits=[("pe", pe_u), ("ac", ac_g)], inc=1)
                dv_fuse = t.snap("dv")
                dep[("gs", ic % 2)] = ("dv", dv_fuse)
                ps_dep[gbank] = ("ac", ac_g)
                ps_dep[ubank] = ("dv", dv_fuse)
            return dv_fuse

        def f2_block(blk, dv_fuse):
            for oc in range(16):
                bank = oc % 4
                gp_q = []
                for hhalf in range(2):
                    slot = 2 * hhalf
                    wl = ([a2_dep[slot]] if a2_dep[slot] else []) +                          ([a2_dep[slot + 1]] if a2_dep[slot + 1] else [])
                    gp_l = t.emit("gp", lambda e, oc=oc, hhalf=hhalf, slot=slot: e.dma_start(
                        out=A2[:, slot * 2048:(slot + 2) * 2048].rearrange(
                            "p (q k c) -> p q k c", q=2, k=NK),
                        in_=wdst[oc, 2 * hhalf:2 * hhalf + 2].rearrange(
                            "q p k c -> p q k c")),
                        waits=wl, incs=[(f"gp{slot}", 16)])
                    gp_q += [gp_l, gp_l]
                xr_dep = xr_guard[oc % 2]
                sy_x = t.emit("sy", lambda e, oc=oc, blk=blk: e.dma_start(
                    out=XR[:, (oc % 2) * 512:(oc % 2) * 512 + 512], in_=xst[blk, oc]),
                    waits=[xr_dep] if xr_dep else [], incs=[(f"syx{oc % 2}", 16)])
                for k in range(64):
                    q = k // NK
                    kk = k % NK
                    t.emit("pe", lambda e, k=k, q=q, kk=kk, bank=bank: e.matmul(
                        out=PS[bank],
                        lhsT=A2[:, q * 2048 + kk * 128:q * 2048 + (kk + 1) * 128],
                        rhs=A1[:, k * TB:(k + 1) * TB],
                        start=(k == 0), stop=(k == 63)),
                        waits=([(f"gp{2 * (q // 2)}", gp_q[q]), ("dv", dv_fuse)]
                               + ([ps_dep[bank]] if ps_dep[bank] else []))
                              if kk == 0 else [],
                        inc=(1 if k == 63 else 0))
                pe_mm = t.snap("pe")
                for q in range(4):
                    a2_dep[q] = ("pe", pe_mm)
                t.emit("ac", lambda e, oc=oc, bank=bank: e.activation(
                    out=GS[:, (oc % 2) * 512:(oc % 2) * 512 + 512], in_=PS[bank],
                    func=AF.Silu, scale=CSD[:, oc:oc + 1]),
                    waits=[("pe", pe_mm)]
                          + ([dep.get(("gs", oc % 2))] if dep.get(("gs", oc % 2)) else []),
                    inc=1)
                ac_g = t.snap("ac")
                ps_dep[bank] = ("ac", ac_g)
                sl = oc % 4
                a3t = A3[:, sl * 512:(sl + 1) * 512]
                t.emit("dv", lambda e, oc=oc: e.tensor_mul(
                    out=GS2, in0=SS[:, oc * TB:(oc + 1) * TB],
                    in1=GS[:, (oc % 2) * 512:(oc % 2) * 512 + 512]),
                    waits=[("ac", ac_g), (f"syx{oc % 2}", sy_x)], inc=0)
                t.emit("dv", lambda e, a3t=a3t, oc=oc: e.tensor_add(
                    out=a3t, in0=GS2, in1=XR[:, (oc % 2) * 512:(oc % 2) * 512 + 512]),
                    waits=a3_guard[sl], inc=1)
                dv_o = t.snap("dv")
                dep[("gs", oc % 2)] = ("dv", dv_o)
                xr_guard[oc % 2] = ("dv", dv_o)
                if oc % 4 == 3:
                    g = oc // 4
                    sy_st = t.emit("sy", lambda e, blk=blk, g=g: e.dma_start(
                        out=outst[blk, 4 * g:4 * g + 4].rearrange("a p t -> p a t"),
                        in_=A3.rearrange("p (a t) -> p a t", a=4)),
                        waits=[("dv", dv_o)], incs=[("syst", 16)])
                    for sl2 in range(4):
                        a3_guard[sl2] = [("syst", sy_st)]
            ss_guard[0] = ("dv", t.snap("dv"))

        for pblk in range(NBLK):
            dv_norm = ln_block(xpst, pblk)
            cols = list(range(0, 16)) + list(range(32, 48)) + [64]
            for ci, cc in enumerate(cols):
                kind = "x" if cc < 16 else ("b" if cc < 48 else "a")
                proj_strip(cc, ci % 4, kind, dv_norm)
            xn_guard[0] = ("pe", t.snap("pe"))
            proj_dv = t.snap("dv")
            for tau in range(4):
                ssd_chunk(tau, summary_only=True, proj_dv=proj_dv)

        if DEBUG:
            t.emit("sy", lambda e: e.dma_start(out=dstf, in_=STF),
                   waits=[("dv", t.snap("dv"))], incs=[("syst", 16)])

        pending_ln = [None]

        for blk in range(NBLK):
            if pending_ln[0] is None:
                dv_norm = ln_block(xst, blk)
            else:
                dv_norm = ln_stats(pending_ln[0])
                pending_ln[0] = None
            cols = (list(range(0, 16)) + list(range(32, 64)) + [64]
                    + list(range(16, 32)))
            for ci, cc in enumerate(cols):
                kind = ("x" if cc < 16 else "z" if cc < 32 else "b" if cc < 48
                        else "c" if cc < 64 else "a")
                proj_strip(cc, ci % 4, kind, dv_norm)
                if cc == 64:
                    la_ac_early = t.snap("ac")
                    proj_dv_early = t.snap("dv")
            proj_dv = proj_dv_early
            if DEBUG and blk == 0:
                t.emit("sy", lambda e: e.dma_start(out=dxn, in_=XN),
                       waits=[("dv", dv_norm)], incs=[("syst", 16)])
                t.emit("sy", lambda e: e.dma_start(out=dlap, in_=LAP),
                       waits=[("ac", t.snap("ac")), ("dv", proj_dv)], incs=[("syst", 16)])
                t.emit("sy", lambda e: e.dma_start(out=da1, in_=A1),
                       waits=[("ac", t.snap("ac")), ("dv", proj_dv)], incs=[("syst", 16)])
            for tau in range(4):
                ssd_chunk(tau, summary_only=False, proj_dv=proj_dv)
            if DEBUG and blk == 0:
                t.emit("sy", lambda e: e.dma_start(out=dncf, in_=NCF),
                       waits=[("dv", t.snap("dv"))], incs=[("syst", 16)])
                t.emit("sy", lambda e: e.dma_start(out=dnct, in_=NCT),
                       waits=[("ac", t.snap("ac"))], incs=[("syst", 16)])
                t.emit("sy", lambda e: e.dma_start(out=dyg, in_=YG),
                       waits=[("dv", t.snap("dv"))], incs=[("syst", 16)])
            dv_yn = yln_block()
            wout_block(dv_yn)
            if DEBUG and blk == 0:
                t.emit("sy", lambda e: e.dma_start(out=dyn, in_=YG),
                       waits=[("dv", dv_yn)], incs=[("syst", 16)])
                t.emit("sy", lambda e: e.dma_start(out=dss, in_=SS),
                       waits=[("ac", t.snap("ac"))], incs=[("syst", 16)])
            dv_fuse = ffn_block(dv_norm)
            if DEBUG and blk == 0:
                t.emit("sy", lambda e: e.dma_start(out=dfuse, in_=A1[:, 0:4096]),
                       waits=[("dv", dv_fuse)], incs=[("syst", 16)])
            xn_guard[0] = ("pe", t.snap("pe"))
            if blk + 1 < NBLK:
                pending_ln[0] = ln_load(xst, blk + 1)
            f2_block(blk, dv_fuse)

        t.emit("sy", lambda e: e.wait_ge(SEM["syst"], t.cnt["syst"]))

        def replay(ek):
            def run(eng):
                hw = {}
                for fn, waits, incs in t.tape[ek]:
                    for (sk, v) in waits:
                        if v > hw.get(sk, 0):
                            eng.wait_ge(SEM[sk], v)
                            hw[sk] = v
                    ins = fn(eng)
                    if incs and ins is not None:
                        for sk, amt in incs:
                            ins = ins.then_inc(SEM[sk], amt)
            return run

        block.sync(replay("sy"))
        block.gpsimd(replay("gp"))
        block.tensor(replay("pe"))
        block.vector(replay("dv"))
        block.scalar(replay("ac"))

    return nc


_NC_CACHE = {}


def _stage_shared(W_in, W_out, gamma, beta, D, w_gate, w_up, w_down, s_gate, s_up, s_down):
    W_in = np.asarray(W_in, np.float32)
    W_in_pad = np.zeros((65 * 128, DM), np.float32)
    W_in_pad[:8224] = W_in
    wst = np.ascontiguousarray(W_in_pad.reshape(65, 128, NK, 128).transpose(0, 3, 2, 1))
    W_out = np.asarray(W_out, np.float32)
    wost = np.ascontiguousarray(W_out.reshape(16, 128, NK, 128).transpose(0, 3, 2, 1))
    wg = np.asarray(w_gate)
    wgst = np.ascontiguousarray(wg.reshape(NK, 128, 64, 128).transpose(2, 1, 0, 3))
    wu = np.asarray(w_up)
    wust = np.ascontiguousarray(wu.reshape(NK, 128, 64, 128).transpose(2, 1, 0, 3))
    wd = np.asarray(w_down)
    wdst = np.ascontiguousarray(
        wd.reshape(4, NK, 128, 16, 128).transpose(3, 0, 2, 1, 4))
    gamv = np.ascontiguousarray(np.asarray(gamma, np.float32).reshape(16, 128).T)
    betv = np.ascontiguousarray(np.asarray(beta, np.float32).reshape(16, 128).T)
    sgv = np.ascontiguousarray(np.asarray(s_gate, np.float32).reshape(64, 128).T)
    suv = np.ascontiguousarray(np.asarray(s_up, np.float32).reshape(64, 128).T)
    sdv = np.ascontiguousarray(np.asarray(s_down, np.float32).reshape(16, 128).T)
    d2 = np.ascontiguousarray(np.tile(np.asarray(D, np.float32).T, (2, 1)))
    nege32 = np.zeros((32, 32 * 128), np.float32)
    for h in range(32):
        nege32[h, h * 128:(h + 1) * 128] = -1.0
    identb = np.eye(128).astype(ml_dtypes.bfloat16)
    identf = np.eye(32, dtype=np.float32)
    maskt = np.tril(np.full((128, 128), np.float32(MASKVAL), dtype=np.float32), k=-1
                    ).astype(ml_dtypes.bfloat16)
    onesb = np.ones((128, 1), ml_dtypes.bfloat16)
    onesr = np.ones((1, 128), np.float32)
    return dict(wst=wst, wost=wost, wgst=wgst, wust=wust, wdst=wdst,
                gam=gamv, bet=betv, sgm=sgv, sum_=suv, sdm=sdv, d2m=d2,
                nege32=nege32, identb=identb, identf=identf,
                maskt=maskt, onesb=onesb, onesr=onesr,
                fconst=np.tile(np.array([[EPS, 1.0 / DM, -0.5, -1.0, 0.0, 1.0, 0.0,
                                          0.0]], np.float32), (128, 1)))


def _stage_x(xslice):
    xT = np.ascontiguousarray(np.asarray(xslice, np.float32).T)
    return np.ascontiguousarray(xT.reshape(NK, 128, NBLK, TB).transpose(2, 0, 1, 3))


def kernel(L0_attn_x, L0_attn_W_in, L0_attn_W_out, L0_attn_gamma, L0_attn_beta,
           L0_attn_D, L0_ffn_w_gate, L0_ffn_w_up, L0_ffn_w_down,
           L0_ffn_s_gate, L0_ffn_s_up, L0_ffn_s_down):
    x = np.asarray(L0_attn_x, np.float32)
    shared = _stage_shared(L0_attn_W_in, L0_attn_W_out, L0_attn_gamma, L0_attn_beta,
                           L0_attn_D, L0_ffn_w_gate, L0_ffn_w_up, L0_ffn_w_down,
                           L0_ffn_s_gate, L0_ffn_s_up, L0_ffn_s_down)
    zeros_prev = np.zeros((NBLK, NK, 128, TB), np.float32)
    in_maps = []
    for c in range(NCORES):
        b, half = c // 2, c % 2
        xs = _stage_x(x[b, half * T:(half + 1) * T, :])
        xp = _stage_x(x[b, 0:T, :]) if half == 1 else zeros_prev
        in_maps.append(dict(xst=xs, xpst=xp, **shared))

    if "nc" not in _NC_CACHE:
        _NC_CACHE["nc"] = build_program()
    nc = _NC_CACHE["nc"]
    res = run_bass_kernel_spmd(nc, in_maps, core_ids=list(range(NCORES)))

    out = np.empty((4, 4096, DM), np.float32)
    for c in range(NCORES):
        b, half = c // 2, c % 2
        o = res.results[c]["outst"]
        ot = np.ascontiguousarray(o.transpose(0, 3, 1, 2)).reshape(T, DM)
        out[b, half * T:(half + 1) * T, :] = ot
    return out


